# version 04
# baseline (speedup 1.0000x reference)
import numpy as np
from contextlib import ExitStack
import concourse.bass as bass
import concourse.mybir as mybir

F32 = mybir.dt.float32
BF16 = mybir.dt.bfloat16
F32R = mybir.dt.float32r
AF = mybir.ActivationFunctionType
ALU = mybir.AluOpType

EPS_L2 = 1e-6


class Prog:
    def __init__(self, name="prog"):
        self.nc = bass.Bass("TRN2", target_bir_lowering=False, debug=False)
        self.ctx = ExitStack()
        self.name = name

    def dram(self, name, shape, dtype=F32, kind="Internal"):
        return self.nc.dram_tensor(name, list(shape), dtype, kind=kind).ap()

    def inp(self, name, shape, dtype=F32):
        return self.dram(name, shape, dtype, kind="ExternalInput")

    def out(self, name, shape, dtype=F32):
        return self.dram(name, shape, dtype, kind="ExternalOutput")

    def sb(self, name, shape, dtype=F32):
        return self.ctx.enter_context(self.nc.sbuf_tensor(name, list(shape), dtype))

    def ps(self, name, shape, dtype=F32):
        return self.ctx.enter_context(self.nc.psum_tensor(name, list(shape), dtype))

    def sem(self, name):
        return self.ctx.enter_context(self.nc.semaphore(name))

    def block(self):
        return self.ctx.enter_context(self.nc.Block())


PE_T, DV_T, AC_T, ST_T = 44, 34, 27, 32


def build_l0_proj(T=4096, C=32):
    p = Prog("l0_proj")
    NT = T // 512

    xT = p.inp("xT", [1024, T])
    wq = p.inp("wq", [1024, 1024])
    wk = p.inp("wk", [1024, 1024])
    wv = p.inp("wv", [1024, 1024])
    wg = p.inp("wg", [1024, 1024])
    wab = p.inp("wab", [1024, 32])
    ba_c = p.inp("ba_c", [16, 1])
    bb_c = p.inp("bb_c", [16, 1])
    ones16 = p.inp("ones16", [1024, 16])
    cmask = p.inp("cmask", [16, T])
    ident = p.inp("ident", [128, 128])

    q_dt = p.out("q_dt", [1024, T], BF16)
    k_dt = p.out("k_dt", [1024, T], BF16)
    g_dt = p.out("g_dt", [1024, T], BF16)
    kt_td = p.out("kt_td", [T, 1024], BF16)
    vt_td = p.out("vt_td", [T, 1024], BF16)
    beta_o = p.out("beta", [16, T])
    gam_o = p.out("gam", [16, T])
    d2f_o = p.out("d2f", [16, T])
    p1f_o = p.out("p1f", [16, T])
    osc_o = p.out("osc", [16, T])
    fT_o = p.out("fT", [T, 48])
    gamC_o = p.out("gamC", [16, T // C])

    wqs = p.sb("wqs", [128, 8 * 1024], BF16)
    wks = p.sb("wks", [128, 8 * 1024], BF16)
    wvs = p.sb("wvs", [128, 8 * 1024], BF16)
    wgs = p.sb("wgs", [128, 8 * 1024], BF16)
    wabs = p.sb("wabs", [128, 8 * 32], BF16)
    ba_s = p.sb("ba_s", [16, 1], F32)
    bb_s = p.sb("bb_s", [16, 1], F32)
    o16s = p.sb("o16s", [128, 8 * 16], F32)
    cmask_s = p.sb("cmask_s", [16, T], F32)
    xb = p.sb("xb", [128, 2 * 8 * 512], BF16)
    ev_q = p.sb("ev_q", [128, 8 * 512], BF16)
    ev_k = p.sb("ev_k", [128, 8 * 512], BF16)
    ev_g = p.sb("ev_g", [128, 8 * 512], BF16)
    ev_t = p.sb("ev_t", [128, 2 * 1024], BF16)
    sq_q = p.sb("sq_q", [128, 8 * 512], F32)
    sq_k = p.sb("sq_k", [128, 8 * 512], F32)
    al_row = p.sb("al_row", [16, T], F32)
    be_row = p.sb("be_row", [16, T], F32)
    la_row = p.sb("la_row", [16, T], F32)
    kkd = p.sb("kkd", [16, T], F32)
    qqd = p.sb("qqd", [16, T], F32)
    fac = p.sb("fac", [16, 8 * T], F32)
    L_, GAM_, NGAM_, SK_, SQd_, NK_, NQ_, TMP_ = range(8)

    def fs(i):
        return fac[:, i * T:(i + 1) * T]

    facout = p.sb("facout", [16, 3 * T], F32)
    idents = p.sb("idents", [128, 128], F32)
    identb = p.sb("identb", [128, 128], BF16)
    fTb = p.sb("fTb", [128, 48], F32)
    gamCs = p.sb("gamCs", [16, T // C], F32)

    ps_a = p.ps("ps_a", [128, 512])
    ps_b = p.ps("ps_b", [128, 512])
    ps_aa = p.ps("ps_aa", [16, 512])
    ps_bb = p.ps("ps_bb", [16, 512])
    pd_k = p.ps("pd_k", [16, 512])
    pd_q = p.ps("pd_q", [16, 512])
    ps_t1 = p.ps("ps_t1", [128, 512])
    ps_t2 = p.ps("ps_t2", [128, 512])

    s_gd = p.sem("s_gd")
    s_pe = p.sem("s_pe")
    s_dv = p.sem("s_dv")
    s_ac = p.sem("s_ac")
    s_st = p.sem("s_st")

    blk = p.block()

    NW_LOAD = 8 * 5 + 2 + 8 + 1 + 1

    def pe_m(tt, g):
        return tt * PE_T + g + 1

    def dv_m(tt, i):
        return tt * DV_T + i + 1

    def ac_m(tt, i):
        return tt * AC_T + i + 1

    def st_m(tt, i):
        return 16 * (tt * ST_T + i + 1)

    @blk.gpsimd
    def _(g):
        for w, ws in ((wq, wqs), (wk, wks), (wv, wvs), (wg, wgs)):
            for j in range(8):
                g.dma_start(out=ws[:, j * 1024:(j + 1) * 1024],
                            in_=w[j * 128:(j + 1) * 128, :]).then_inc(s_gd, 16)
        for j in range(8):
            g.dma_start(out=wabs[:, j * 32:(j + 1) * 32],
                        in_=wab[j * 128:(j + 1) * 128, :]).then_inc(s_gd, 16)
        g.dma_start(out=ba_s[:], in_=ba_c).then_inc(s_gd, 16)
        g.dma_start(out=bb_s[:], in_=bb_c).then_inc(s_gd, 16)
        for j in range(8):
            g.dma_start(out=o16s[:, j * 16:(j + 1) * 16],
                        in_=ones16[j * 128:(j + 1) * 128, :]).then_inc(s_gd, 16)
        g.dma_start(out=cmask_s[:], in_=cmask).then_inc(s_gd, 16)
        g.dma_start(out=idents[:], in_=ident).then_inc(s_gd, 16)
        for tt in range(NT):
            if tt >= 2:
                g.wait_ge(s_pe, pe_m(tt - 2, PE_T - 1))
            for j in range(8):
                g.dma_start(out=xb[:, (tt % 2) * 4096 + j * 512:(tt % 2) * 4096 + j * 512 + 512],
                            in_=xT[j * 128:(j + 1) * 128, tt * 512:(tt + 1) * 512]).then_inc(s_gd, 16)

    def xsl(tt, dj, lo, n):
        base = (tt % 2) * 4096 + dj * 512 + lo
        return xb[:, base:base + n]

    @blk.tensor
    def _(t):
        for tt in range(NT):
            t.wait_ge(s_gd, 16 * (NW_LOAD + 8 * (tt + 1)))
            for pi, ws in ((0, wqs), (1, wks), (2, wgs)):
                for oc in range(8):
                    gidx = pi * 8 + oc
                    pst = ps_a if oc % 2 == 0 else ps_b
                    if gidx >= 2:
                        pg, poc = divmod(gidx - 2, 8)
                        if pg == 0:
                            t.wait_ge(s_dv, dv_m(tt, poc))
                            t.wait_ge(s_ac, ac_m(tt, poc))
                        elif pg == 1:
                            t.wait_ge(s_dv, dv_m(tt, 8 + poc))
                            t.wait_ge(s_ac, ac_m(tt, 8 + poc))
                        else:
                            t.wait_ge(s_ac, ac_m(tt, 16 + poc))
                    elif tt > 0:
                        t.wait_ge(s_ac, ac_m(tt - 1, 16 + 6 + gidx))
                    for dj in range(8):
                        mm = t.matmul(pst[:, :],
                                      ws[:, dj * 1024 + oc * 128:dj * 1024 + oc * 128 + 128],
                                      xsl(tt, dj, 0, 512),
                                      start=(dj == 0), stop=(dj == 7))
                        if dj == 7:
                            mm.then_inc(s_pe, 1)
            for bi, pst in ((0, ps_aa), (1, ps_bb)):
                if tt > 0:
                    t.wait_ge(s_ac, ac_m(tt - 1, 24 + bi))
                for dj in range(8):
                    mm = t.matmul(pst[:, :],
                                  wabs[:, dj * 32 + bi * 16:dj * 32 + bi * 16 + 16],
                                  xsl(tt, dj, 0, 512),
                                  start=(dj == 0), stop=(dj == 7))
                    if dj == 7:
                        mm.then_inc(s_pe, 1)
            t.wait_ge(s_ac, ac_m(tt, 15))
            if tt > 0:
                t.wait_ge(s_dv, dv_m(tt - 1, 16))
            for oc in range(8):
                mm = t.matmul(pd_k[:, :], o16s[:, oc * 16:(oc + 1) * 16],
                              sq_k[:, oc * 512:(oc + 1) * 512],
                              start=(oc == 0), stop=(oc == 7))
                if oc == 7:
                    mm.then_inc(s_pe, 1)
            if tt > 0:
                t.wait_ge(s_dv, dv_m(tt - 1, 17))
            for oc in range(8):
                mm = t.matmul(pd_q[:, :], o16s[:, oc * 16:(oc + 1) * 16],
                              sq_q[:, oc * 512:(oc + 1) * 512],
                              start=(oc == 0), stop=(oc == 7))
                if oc == 7:
                    mm.then_inc(s_pe, 1)
            for tc in range(4):
                for si, ws in ((0, wks), (1, wvs)):
                    for half in range(2):
                        tg = tc * 4 + si * 2 + half
                        pst = ps_t1 if tg % 2 == 0 else ps_t2
                        if tg >= 2:
                            t.wait_ge(s_dv, dv_m(tt, 18 + tg - 2))
                        elif tt > 0:
                            t.wait_ge(s_dv, dv_m(tt - 1, 18 + 14 + tg))
                        for dj in range(8):
                            mm = t.matmul(pst[:, :], xsl(tt, dj, tc * 128, 128),
                                          ws[:, dj * 1024 + half * 512:dj * 1024 + half * 512 + 512],
                                          start=(dj == 0), stop=(dj == 7))
                            if dj == 7:
                                mm.then_inc(s_pe, 1)
        NB = T // 128
        t.wait_ge(s_dv, NT * DV_T + 10)
        for b in range(NB):
            for f in range(3):
                if b + f > 0:
                    t.wait_ge(s_dv, NT * DV_T + 11 + (b * 3 + f - 1) + 1)
                src_sl = (facout[:, 2 * T + b * 128:2 * T + b * 128 + 128] if f == 0
                          else facout[:, T + b * 128:T + b * 128 + 128] if f == 1
                          else facout[:, b * 128:b * 128 + 128])
                t.transpose(ps_t1[0:128, 0:16], src_sl, idents[0:16, 0:16]).then_inc(s_pe, 1)

    @blk.vector
    def _(v):
        for tt in range(NT):
            for oc in range(8):
                v.wait_ge(s_pe, pe_m(tt, oc))
                if tt > 0 and oc == 0:
                    v.wait_ge(s_st, st_m(tt - 1, 7))
                pst = ps_a if oc % 2 == 0 else ps_b
                v.tensor_copy(ev_q[:, oc * 512:(oc + 1) * 512], pst[:, :]).then_inc(s_dv, 1)
            for oc in range(8):
                v.wait_ge(s_pe, pe_m(tt, 8 + oc))
                if tt > 0 and oc == 0:
                    v.wait_ge(s_st, st_m(tt - 1, 15))
                pst = ps_a if oc % 2 == 0 else ps_b
                v.tensor_copy(ev_k[:, oc * 512:(oc + 1) * 512], pst[:, :]).then_inc(s_dv, 1)
            v.wait_ge(s_pe, pe_m(tt, 26))
            v.tensor_copy(kkd[:, tt * 512:(tt + 1) * 512], pd_k[:, :]).then_inc(s_dv, 1)
            v.wait_ge(s_pe, pe_m(tt, 27))
            v.tensor_copy(qqd[:, tt * 512:(tt + 1) * 512], pd_q[:, :]).then_inc(s_dv, 1)
            for tg in range(16):
                tc, r = divmod(tg, 4)
                si, half = divmod(r, 2)
                v.wait_ge(s_pe, pe_m(tt, 28 + tg))
                if tc > 0:
                    v.wait_ge(s_st, st_m(tt, 24 + (tc - 1) * 2 + si))
                elif tt > 0:
                    v.wait_ge(s_st, st_m(tt - 1, 24 + 3 * 2 + si))
                pst = ps_t1 if tg % 2 == 0 else ps_t2
                v.tensor_copy(ev_t[:, si * 1024 + half * 512:si * 1024 + half * 512 + 512],
                              pst[:, :]).then_inc(s_dv, 1)
        nd = NT * DV_T
        v.wait_ge(s_ac, ac_m(NT - 1, 26))
        v.tensor_tensor_scan(fs(L_), cmask_s[:, :], la_row[:, :], 0.0,
                             ALU.mult, ALU.add).then_inc(s_dv, 1)
        v.wait_ge(s_ac, NT * AC_T + 2)
        v.tensor_scalar_max(fs(NK_), fs(SK_), EPS_L2).then_inc(s_dv, 1)
        v.wait_ge(s_dv, nd + 2)
        v.reciprocal(fs(NK_), fs(NK_)).then_inc(s_dv, 1)
        v.tensor_scalar_max(fs(NQ_), fs(SQd_), EPS_L2).then_inc(s_dv, 1)
        v.wait_ge(s_dv, nd + 4)
        v.reciprocal(fs(NQ_), fs(NQ_)).then_inc(s_dv, 1)
        v.wait_ge(s_dv, nd + 5)
        v.tensor_tensor(fs(TMP_), fs(NK_), fs(NK_), ALU.mult).then_inc(s_dv, 1)
        v.wait_ge(s_dv, nd + 6)
        v.tensor_tensor(facout[:, 0:T], fs(TMP_), be_row[:, :], ALU.mult).then_inc(s_dv, 1)
        v.tensor_tensor(fs(TMP_), fs(NK_), be_row[:, :], ALU.mult).then_inc(s_dv, 1)
        v.wait_ge(s_ac, NT * AC_T + 4)
        v.wait_ge(s_dv, nd + 8)
        v.tensor_tensor(facout[:, T:2 * T], fs(TMP_), fs(NGAM_), ALU.mult).then_inc(s_dv, 1)
        v.tensor_tensor(facout[:, 2 * T:3 * T], fs(GAM_), fs(NQ_), ALU.mult).then_inc(s_dv, 1)
        v.tensor_copy(gamCs[:, :],
                      fs(GAM_).rearrange("h (c k) -> h c k", k=C)[:, :, C - 1:C]).then_inc(s_dv, 1)
        NB = T // 128
        for b in range(NB):
            for f in range(3):
                v.wait_ge(s_pe, NT * PE_T + (b * 3 + f) + 1)
                if f == 2:
                    v.tensor_scalar(fTb[:, f * 16:f * 16 + 16], ps_t1[0:128, 0:16],
                                    -1.0, None, ALU.mult).then_inc(s_dv, 1)
                else:
                    v.tensor_copy(fTb[:, f * 16:f * 16 + 16], ps_t1[0:128, 0:16]).then_inc(s_dv, 1)
            if b + 1 < NB:
                v.wait_ge(s_st, 16 * (NT * ST_T + 5 + b + 1))

    @blk.scalar
    def _(a):
        for tt in range(NT):
            for oc in range(8):
                a.wait_ge(s_pe, pe_m(tt, oc))
                a.wait_ge(s_dv, dv_m(tt, oc))
                pst = ps_a if oc % 2 == 0 else ps_b
                a.activation(sq_q[:, oc * 512:(oc + 1) * 512], pst[:, :],
                             AF.Square).then_inc(s_ac, 1)
            for oc in range(8):
                a.wait_ge(s_pe, pe_m(tt, 8 + oc))
                a.wait_ge(s_dv, dv_m(tt, 8 + oc))
                pst = ps_a if oc % 2 == 0 else ps_b
                a.activation(sq_k[:, oc * 512:(oc + 1) * 512], pst[:, :],
                             AF.Square).then_inc(s_ac, 1)
            for oc in range(8):
                a.wait_ge(s_pe, pe_m(tt, 16 + oc))
                if tt > 0 and oc == 0:
                    a.wait_ge(s_st, st_m(tt - 1, 23))
                pst = ps_a if oc % 2 == 0 else ps_b
                a.activation(ev_g[:, oc * 512:(oc + 1) * 512], pst[:, :],
                             AF.Silu).then_inc(s_ac, 1)
            a.wait_ge(s_pe, pe_m(tt, 24))
            a.activation(al_row[:, tt * 512:(tt + 1) * 512], ps_aa[:, :],
                         AF.Sigmoid, bias=ba_s[:, 0:1]).then_inc(s_ac, 1)
            a.wait_ge(s_pe, pe_m(tt, 25))
            a.activation(be_row[:, tt * 512:(tt + 1) * 512], ps_bb[:, :],
                         AF.Sigmoid, bias=bb_s[:, 0:1]).then_inc(s_ac, 1)
            a.wait_ge(s_ac, ac_m(tt, 24))
            a.activation(la_row[:, tt * 512:(tt + 1) * 512],
                         al_row[:, tt * 512:(tt + 1) * 512], AF.Ln).then_inc(s_ac, 1)
        na = NT * AC_T
        a.wait_ge(s_dv, dv_m(NT - 1, 17))
        a.activation(fs(SK_), kkd[:, :], AF.Sqrt).then_inc(s_ac, 1)
        a.activation(fs(SQd_), qqd[:, :], AF.Sqrt).then_inc(s_ac, 1)
        a.wait_ge(s_dv, NT * DV_T + 1)
        a.activation(fs(GAM_), fs(L_), AF.Exp).then_inc(s_ac, 1)
        a.activation(fs(NGAM_), fs(L_), AF.Exp, scale=-1.0).then_inc(s_ac, 1)

    @blk.sync
    def _(s):
        for tt in range(NT):
            s.wait_ge(s_dv, dv_m(tt, 7))
            for j in range(8):
                s.dma_start(out=q_dt[j * 128:(j + 1) * 128, tt * 512:(tt + 1) * 512],
                            in_=ev_q[:, j * 512:(j + 1) * 512]).then_inc(s_st, 16)
            s.wait_ge(s_dv, dv_m(tt, 15))
            for j in range(8):
                s.dma_start(out=k_dt[j * 128:(j + 1) * 128, tt * 512:(tt + 1) * 512],
                            in_=ev_k[:, j * 512:(j + 1) * 512]).then_inc(s_st, 16)
            s.wait_ge(s_ac, ac_m(tt, 23))
            for j in range(8):
                s.dma_start(out=g_dt[j * 128:(j + 1) * 128, tt * 512:(tt + 1) * 512],
                            in_=ev_g[:, j * 512:(j + 1) * 512]).then_inc(s_st, 16)
            for tc in range(4):
                r0 = tt * 512 + tc * 128
                s.wait_ge(s_dv, dv_m(tt, 18 + tc * 4 + 1))
                s.dma_start(out=kt_td[r0:r0 + 128, :],
                            in_=ev_t[:, 0:1024]).then_inc(s_st, 16)
                s.wait_ge(s_dv, dv_m(tt, 18 + tc * 4 + 3))
                s.dma_start(out=vt_td[r0:r0 + 128, :],
                            in_=ev_t[:, 1024:2048]).then_inc(s_st, 16)
        s.wait_ge(s_dv, NT * DV_T + 10)
        s.dma_start(out=beta_o, in_=be_row[:, :]).then_inc(s_st, 16)
        s.dma_start(out=gam_o, in_=fs(GAM_)).then_inc(s_st, 16)
        s.dma_start(out=d2f_o, in_=facout[:, 0:T]).then_inc(s_st, 16)
        s.dma_start(out=p1f_o, in_=facout[:, T:2 * T]).then_inc(s_st, 16)
        s.dma_start(out=osc_o, in_=facout[:, 2 * T:3 * T]).then_inc(s_st, 16)
        NB = T // 128
        for b in range(NB):
            s.wait_ge(s_dv, NT * DV_T + 11 + (b * 3 + 3))
            s.dma_start(out=fT_o[b * 128:(b + 1) * 128, :], in_=fTb[:, :]).then_inc(s_st, 16)
        s.wait_ge(s_dv, NT * DV_T + 11)
        s.dma_start(out=gamC_o, in_=gamCs[:, :]).then_inc(s_st, 16)
        s.wait_ge(s_st, 16 * (NT * ST_T + 5 + NB + 1))

    p.ctx.close()
    return p.nc


def host_l0_proj_inputs(x, Wq, Wk, Wv, Wg, Wa, ba, Wb, bb, T=4096, C=32):
    d = {}
    d["xT"] = np.ascontiguousarray(x.T).astype(np.float32)
    d["wq"] = np.ascontiguousarray(Wq.T).astype(np.float32)
    d["wk"] = np.ascontiguousarray(Wk.T).astype(np.float32)
    d["wv"] = np.ascontiguousarray(Wv.T).astype(np.float32)
    d["wg"] = np.ascontiguousarray(Wg.T).astype(np.float32)
    d["wab"] = np.ascontiguousarray(
        np.concatenate([Wa.T, Wb.T], axis=1)).astype(np.float32)
    d["ba_c"] = ba.reshape(16, 1).astype(np.float32)
    d["bb_c"] = bb.reshape(16, 1).astype(np.float32)
    o = np.zeros((1024, 16), np.float32)
    for dd in range(1024):
        o[dd, dd // 64] = 1.0
    d["ones16"] = o
    cm = np.ones((16, T), np.float32)
    cm[:, ::C] = 0.0
    d["cmask"] = cm
    d["ident"] = np.eye(128, dtype=np.float32)
    return d


def build_l0_scan(T=4096, C=32):
    p = Prog("l0_scan")
    NCH = T // C
    SC = 256
    NSC = T // SC
    CPS = SC // C
    NIT = C - 1

    q_dt = p.inp("q_dt", [1024, T], BF16)
    k_dt = p.inp("k_dt", [1024, T], BF16)
    kt_td = p.inp("kt_td", [T, 1024], BF16)
    vt_td = p.inp("vt_td", [T, 1024], BF16)
    fT = p.inp("fT", [T, 48])
    gamC = p.inp("gamC", [16, NCH])
    trilS = p.inp("trilS", [128, 512])
    triuI = p.inp("triuI", [128, 512])
    ind2 = p.inp("ind2", [16, 128])
    sel8 = p.inp("sel8", [16, 8])
    oT = p.out("oT", [T, 1024], BF16)

    kb = p.sb("kb", [128, 8 * 2 * SC], BF16)
    qb = p.sb("qb", [128, 8 * 2 * SC], BF16)
    fb = p.sb("fb", [128, 2 * 2 * 48], F32)
    vtc = p.sb("vtc", [128, 2 * 256], BF16)
    ktc = p.sb("ktc", [128, 2 * 256], BF16)
    ktcz = p.sb("ktcz", [128, 2 * 1024], BF16)
    gamCs = p.sb("gamCs", [16, NCH], F32)
    trilSs = p.sb("trilSs", [128, 512], F32)
    triuIs = p.sb("triuIs", [128, 512], F32)
    ind2s = p.sb("ind2s", [16, 128], F32)
    sel8s = p.sb("sel8s", [16, 8], F32)
    mblk = p.sb("mblk", [128, 512], F32)
    qkm = p.sb("qkm", [128, 512], BF16)
    zb = p.sb("zb", [128, 512], BF16)
    xmb = p.sb("xmb", [128, 256], BF16)
    fcols = p.sb("fcols", [128, 12], F32)
    rb = p.sb("rb", [128, 256], F32)
    xm = p.sb("xm", [128, 256], F32)
    zs = p.sb("zs", [128, 512], F32)
    ztmp = p.sb("ztmp", [128, 512], F32)
    gcp = p.sb("gcp", [128, 8], F32)
    gcr = p.sb("gcr", [16, 8], F32)
    oxm = p.sb("oxm", [128, 256], F32)

    ps_m = p.ps("ps_m", [128, 512])
    ps_qk = p.ps("ps_qk", [128, 512])
    ps_gk = p.ps("ps_gk", [128, 512])
    ps_o = p.ps("ps_o", [128, 512])
    ps_h1 = p.ps("ps_h1", [128, 512])
    ps_h2 = p.ps("ps_h2", [128, 512])
    ps_z = p.ps("ps_z", [128, 512])
    ps_g = p.ps("ps_g", [128, 512])

    s_ld = p.sem("s_ld")
    s_pe = p.sem("s_pe")
    s_dv = p.sem("s_dv")
    s_gs = p.sem("s_gs")

    blk = p.block()

    NCONST = 5
    LD_SC = 18
    LD_CH = 48

    def ld_m(c):
        sc = c // CPS
        return 16 * (NCONST + (sc + 1) * LD_SC + (c + 1) * LD_CH)

    PE_C = 6 + NIT
    G_KK, G_QK, G_GK, G_GC = 0, 1, 2, 3
    G_H0 = 4
    G_O = 4 + NIT
    G_Z = 5 + NIT

    def pe_m(c, g):
        return c * PE_C + g + 1

    D_GCR, D_MB, D_QKM = 0, 1, 2
    D_FC0 = 3
    D_RB0 = 3 + 48
    D_H0 = D_RB0 + 8
    D_XMB = D_H0 + 4 * NIT
    D_OTS = D_XMB + 1
    D_GCP = D_OTS + 4
    D_ZT = D_GCP + 1
    D_ZTS0 = D_ZT + 1
    D_ZB = D_ZTS0 + 8
    DV_C = D_ZB + 1

    def dv_m(c, i):
        return 3 + c * DV_C + i + 1

    GS_C = 16

    def gs_m(c, i):
        return 16 * (c * GS_C + i + 1)

    @blk.sync
    def _(s):
        s.dma_start(out=trilSs[:], in_=trilS).then_inc(s_ld, 16)
        s.dma_start(out=triuIs[:], in_=triuI).then_inc(s_ld, 16)
        s.dma_start(out=ind2s[:], in_=ind2).then_inc(s_ld, 16)
        s.dma_start(out=sel8s[:], in_=sel8).then_inc(s_ld, 16)
        s.dma_start(out=gamCs[:], in_=gamC).then_inc(s_ld, 16)
        for c in range(NCH):
            sc, ci = divmod(c, CPS)
            par = sc % 2
            if ci == 0:
                if sc >= 2:
                    s.wait_ge(s_pe, pe_m((sc - 1) * CPS - 1, PE_C - 1))
                for j in range(8):
                    s.dma_start(out=kb[:, (j * 2 + par) * SC:(j * 2 + par) * SC + SC],
                                in_=k_dt[j * 128:(j + 1) * 128, sc * SC:(sc + 1) * SC]).then_inc(s_ld, 16)
                for j in range(8):
                    s.dma_start(out=qb[:, (j * 2 + par) * SC:(j * 2 + par) * SC + SC],
                                in_=q_dt[j * 128:(j + 1) * 128, sc * SC:(sc + 1) * SC]).then_inc(s_ld, 16)
                for hf in range(2):
                    s.dma_start(out=fb[:, (par * 2 + hf) * 48:(par * 2 + hf) * 48 + 48],
                                in_=fT[sc * SC + hf * 128:sc * SC + hf * 128 + 128, :]).then_inc(s_ld, 16)
            t0 = c * C
            cpar = c % 2
            if c >= 2:
                s.wait_ge(s_pe, pe_m(c - 2, PE_C - 1))
            for h in range(16):
                g, i = divmod(h, 4)
                s.dma_start(out=vtc[32 * i:32 * i + 32, cpar * 256 + 64 * g:cpar * 256 + 64 * g + 64],
                            in_=vt_td[t0:t0 + C, 64 * h:64 * h + 64]).then_inc(s_ld, 16)
            for h in range(16):
                g, i = divmod(h, 4)
                s.dma_start(out=ktc[32 * i:32 * i + 32, cpar * 256 + 64 * g:cpar * 256 + 64 * g + 64],
                            in_=kt_td[t0:t0 + C, 64 * h:64 * h + 64]).then_inc(s_ld, 16)
            for h in range(16):
                g, i = divmod(h, 4)
                sidx = 2 * g + (i // 2)
                s.dma_start(out=ktcz[32 * i:32 * i + 32,
                                     cpar * 1024 + 128 * sidx + 64 * (i % 2):cpar * 1024 + 128 * sidx + 64 * (i % 2) + 64],
                            in_=kt_td[t0:t0 + C, 64 * h:64 * h + 64]).then_inc(s_ld, 16)

    def kchunk(c, h):
        sc, ci = divmod(c, CPS)
        par = sc % 2
        j = h // 2
        base = (j * 2 + par) * SC + ci * C
        return kb[64 * (h % 2):64 * (h % 2) + 64, base:base + C]

    def qchunk(c, h):
        sc, ci = divmod(c, CPS)
        par = sc % 2
        j = h // 2
        base = (j * 2 + par) * SC + ci * C
        return qb[64 * (h % 2):64 * (h % 2) + 64, base:base + C]

    def zsl(h):
        return zs[64 * (h % 2):64 * (h % 2) + 64, 64 * (h // 2):64 * (h // 2) + 64]

    def zbsl(h):
        return zb[64 * (h % 2):64 * (h % 2) + 64, 64 * (h // 2):64 * (h // 2) + 64]

    def prev_h_bank_iter(m):
        return NIT - 1 if (NIT - 1) % 2 == m % 2 else NIT - 2

    @blk.tensor
    def _(t):
        for c in range(NCH):
            t.wait_ge(s_ld, ld_m(c))
            if c == 0:
                t.wait_ge(s_dv, 3)
            else:
                t.wait_ge(s_dv, dv_m(c - 1, DV_C - 1))
            for h in range(16):
                g, i = divmod(h, 4)
                mm = t.matmul(ps_m[32 * i:32 * i + 32, 128 * g + 32 * i:128 * g + 32 * i + 32],
                              kchunk(c, h), kchunk(c, h), start=True, stop=True,
                              tile_position=(64 * (h % 2), 32 * i))
                if h == 15:
                    mm.then_inc(s_pe, 1)
            for h in range(16):
                g, i = divmod(h, 4)
                mm = t.matmul(ps_qk[32 * i:32 * i + 32, 128 * g + 32 * i:128 * g + 32 * i + 32],
                              kchunk(c, h), qchunk(c, h), start=True, stop=True,
                              tile_position=(64 * (h % 2), 32 * i))
                if h == 15:
                    mm.then_inc(s_pe, 1)
            for h in range(16):
                g, i = divmod(h, 4)
                mm = t.matmul(ps_gk[32 * i:32 * i + 32, 64 * g:64 * g + 64],
                              kchunk(c, h), zbsl(h), start=True, stop=True,
                              tile_position=(64 * (h % 2), 32 * i))
                if h == 15:
                    mm.then_inc(s_pe, 1)
            t.wait_ge(s_dv, dv_m(c, D_GCR))
            mm = t.matmul(ps_g[0:128, 0:8], ind2s[:, :], gcr[:, :],
                          start=True, stop=True)
            mm.then_inc(s_pe, 1)
            for m in range(NIT):
                psx = ps_h1 if m % 2 == 0 else ps_h2
                if m == 0:
                    t.wait_ge(s_dv, dv_m(c, D_RB0 + 7))
                else:
                    t.wait_ge(s_dv, dv_m(c, D_H0 + 4 * m - 1))
                if m >= 2:
                    t.wait_ge(s_dv, dv_m(c, D_H0 + 4 * (m - 1) - 1))
                rhs = rb if m == 0 else xm
                for g in range(4):
                    mm = t.matmul(psx[:, 64 * g:64 * g + 64],
                                  mblk[:, 128 * g:128 * g + 128],
                                  rhs[:, 64 * g:64 * g + 64],
                                  start=True, stop=True)
                    if g == 3:
                        mm.then_inc(s_pe, 1)
            t.wait_ge(s_dv, dv_m(c, D_XMB))
            for h in range(16):
                g, i = divmod(h, 4)
                t.matmul(ps_o[32 * i:32 * i + 32, 64 * g:64 * g + 64],
                         qchunk(c, h), zbsl(h), start=True, stop=False,
                         tile_position=(64 * (h % 2), 32 * i))
                mm = t.matmul(ps_o[32 * i:32 * i + 32, 64 * g:64 * g + 64],
                              qkm[32 * i:32 * i + 32, 128 * g + 32 * i:128 * g + 32 * i + 32],
                              xmb[32 * i:32 * i + 32, 64 * g:64 * g + 64],
                              start=False, stop=True,
                              tile_position=(32 * i, 32 * i))
                if h == 15:
                    mm.then_inc(s_pe, 1)
            cpar = c % 2
            for sidx in range(8):
                g = sidx // 2
                mm = t.matmul(ps_z[:, 64 * sidx:64 * sidx + 64],
                              ktcz[:, cpar * 1024 + 128 * sidx:cpar * 1024 + 128 * sidx + 128],
                              xmb[:, 64 * g:64 * g + 64],
                              start=True, stop=True)
                if sidx == 7:
                    mm.then_inc(s_pe, 1)

    @blk.vector
    def _(v):
        v.memset(zs[:, :], 0.0)
        v.nop().then_inc(s_dv, 1)
        v.memset(zb[:, :], 0.0)
        v.nop().then_inc(s_dv, 1)
        v.memset(ktcz[:, :], 0.0)
        v.nop().then_inc(s_dv, 1)
        for c in range(NCH):
            v.wait_ge(s_ld, 16 * NCONST)
            if c > 0:
                v.wait_ge(s_pe, pe_m(c - 1, G_GC))
            v.tensor_tensor(gcr[:, :],
                            gamCs[:, c:c + 1].broadcast_to((16, 8)),
                            sel8s[:, :], ALU.mult).then_inc(s_dv, 1)
            v.wait_ge(s_pe, pe_m(c, G_KK))
            v.tensor_tensor(mblk[:, :], ps_m[:, :], trilSs[:, :], ALU.mult).then_inc(s_dv, 1)
            v.wait_ge(s_pe, pe_m(c, G_QK))
            v.tensor_tensor(qkm[:, :], ps_qk[:, :], triuIs[:, :], ALU.mult).then_inc(s_dv, 1)
            v.wait_ge(s_ld, ld_m(c))
            sc_, ci_ = divmod(c, CPS)
            par_ = sc_ % 2
            hf0_ = (ci_ * C) // 128
            r0_ = (c * C) % 128
            for f in (2, 1, 0):
                for h in range(16):
                    gi_, i_ = divmod(h, 4)
                    col_ = (par_ * 2 + hf0_) * 48 + f * 16 + h
                    cp = v.tensor_copy(fcols[32 * i_:32 * i_ + 32, f * 4 + gi_:f * 4 + gi_ + 1],
                                       fb[r0_:r0_ + 32, col_:col_ + 1])
                    if f == 0 and h == 15:
                        cp.then_inc(s_dv, 48)
            v.wait_ge(s_pe, pe_m(c, G_GK))
            cpar = c % 2
            for g in range(4):
                v.tensor_scalar(rb[:, 64 * g:64 * g + 64],
                                vtc[:, cpar * 256 + 64 * g:cpar * 256 + 64 * g + 64],
                                fcols[:, 4 + g:4 + g + 1], None,
                                ALU.mult)
                op = v.scalar_tensor_tensor(rb[:, 64 * g:64 * g + 64],
                                            ps_gk[:, 64 * g:64 * g + 64],
                                            fcols[:, 8 + g:8 + g + 1],
                                            rb[:, 64 * g:64 * g + 64],
                                            ALU.mult, ALU.add)
                if g == 3:
                    op.then_inc(s_dv, 8)
            for m in range(NIT):
                psx = ps_h1 if m % 2 == 0 else ps_h2
                v.wait_ge(s_pe, pe_m(c, G_H0 + m))
                for g in range(4):
                    op = v.scalar_tensor_tensor(xm[:, 64 * g:64 * g + 64],
                                                psx[:, 64 * g:64 * g + 64],
                                                fcols[:, 8 + g:8 + g + 1],
                                                rb[:, 64 * g:64 * g + 64],
                                                ALU.mult, ALU.add)
                    if g == 3:
                        op.then_inc(s_dv, 4)
            v.tensor_copy(xmb[:, :], xm[:, :]).then_inc(s_dv, 1)
            v.wait_ge(s_pe, pe_m(c, G_O))
            if c > 0:
                v.wait_ge(s_gs, gs_m(c - 1, GS_C - 1))
            for g in range(4):
                op = v.tensor_scalar(oxm[:, 64 * g:64 * g + 64],
                                     ps_o[:, 64 * g:64 * g + 64],
                                     fcols[:, g:g + 1], None, ALU.mult)
                if g == 3:
                    op.then_inc(s_dv, 4)
            v.wait_ge(s_pe, pe_m(c, G_GC))
            v.tensor_copy(gcp[:, :], ps_g[0:128, 0:8]).then_inc(s_dv, 1)
            v.wait_ge(s_pe, pe_m(c, G_Z))
            v.tensor_tensor(ztmp[:, :], zs[:, :], ps_z[:, :], ALU.add).then_inc(s_dv, 1)
            for j in range(8):
                v.tensor_scalar(zs[:, 64 * j:64 * j + 64], ztmp[:, 64 * j:64 * j + 64],
                                gcp[:, j:j + 1], None, ALU.mult)
            v.tensor_copy(zb[:, :], zs[:, :]).then_inc(s_dv, 9)

    @blk.gpsimd
    def _(g):
        for c in range(NCH):
            g.wait_ge(s_dv, dv_m(c, D_OTS + 3))
            t0 = c * C
            for h in range(16):
                gi, i = divmod(h, 4)
                g.dma_start(out=oT[t0:t0 + C, 64 * h:64 * h + 64],
                            in_=oxm[32 * i:32 * i + 32, 64 * gi:64 * gi + 64]).then_inc(s_gs, 16)
        g.wait_ge(s_gs, 16 * NCH * GS_C)

    p.ctx.close()
    return p.nc


def host_l0_scan_consts(T=4096, C=32):
    trilS = np.zeros((128, 512), np.float32)
    triuI = np.zeros((128, 512), np.float32)
    for g in range(4):
        for i in range(4):
            for s in range(32):
                for t in range(32):
                    if s < t:
                        trilS[32 * i + s, 128 * g + 32 * i + t] = 1.0
                    if s <= t:
                        triuI[32 * i + s, 128 * g + 32 * i + t] = 1.0
    ind2 = np.zeros((16, 128), np.float32)
    for h in range(16):
        ind2[h, 64 * (h % 2):64 * (h % 2) + 64] = 1.0
    sel8 = np.zeros((16, 8), np.float32)
    for h in range(16):
        sel8[h, h // 2] = 1.0
    return dict(trilS=trilS, triuI=triuI, ind2=ind2, sel8=sel8)


def build_l0_post(T=4096, name="l0_post", gate=True):
    p = Prog(name)
    NT = T // 512

    oT = p.inp("oT", [T, 1024], BF16)
    g_dt = p.inp("g_dt", [1024, T], BF16)
    xT = p.inp("xT", [1024, T])
    woT = p.inp("woT", [1024, 1024])
    onescol = p.inp("onescol", [1024, 1])
    onesrow = p.inp("onesrow", [1, 128])
    x1T = p.out("x1T", [1024, T])
    x1b = p.out("x1b", [1024, T], BF16)

    wos = p.sb("wos", [128, 8 * 1024], BF16)
    ob = p.sb("ob", [128, 8 * 512], BF16)
    gb = p.sb("gb", [128, 8 * 512], BF16)
    xb = p.sb("xb", [128, 8 * 512], F32)
    og = p.sb("og", [128, 8 * 512], BF16)
    xr = p.sb("xr", [128, 8 * 512], F32)
    xrb = p.sb("xrb", [128, 8 * 512], BF16)
    sqb = p.sb("sqb", [128, 8 * 512], BF16)
    ocol = p.sb("ocol", [128, 1], BF16)
    orow = p.sb("orow", [1, 128], F32)
    srow = p.sb("srow", [1, 6 * 512], F32)
    tmp = p.sb("tmp", [128, 8 * 512], F32)
    x1s = p.sb("x1s", [128, 8 * 512], F32)
    x1bs = p.sb("x1bs", [128, 8 * 512], BF16)

    ps_a = p.ps("ps_a", [128, 512])
    ps_b = p.ps("ps_b", [128, 512])
    pd_s = p.ps("pd_s", [128, 512])
    pd_q = p.ps("pd_q", [128, 512])
    ps_b1 = p.ps("ps_b1", [128, 512])
    ps_b2 = p.ps("ps_b2", [128, 512])

    s_ld = p.sem("s_ld")
    s_pe = p.sem("s_pe")
    s_dv = p.sem("s_dv")
    s_ac = p.sem("s_ac")
    s_st = p.sem("s_st")

    blk = p.block()

    NCONST = 10
    LD_T = 48 if gate else 40

    def ld_m(tt):
        return 16 * (NCONST + (tt + 1) * LD_T)

    PE_T = 12

    def pe_m(tt, g):
        return tt * PE_T + g + 1

    DV_T = 50
    DC_OG = 8

    def DC_XR(oc):
        return 8 + oc + 1
    DC_XRB = 24
    DC_VAR = 25
    DC_NM = 26

    def DC_AP(j):
        return 26 + 3 * (j + 1)

    def dv_c(tt, cnt):
        return tt * DV_T + cnt

    AC_T = 12

    def ac_c(tt, cnt):
        return tt * AC_T + cnt

    ST_T = 16

    def st_m(tt, i):
        return 16 * (tt * ST_T + i + 1)

    @blk.gpsimd
    def _(g):
        for j in range(8):
            g.dma_start(out=wos[:, j * 1024:(j + 1) * 1024],
                        in_=woT[j * 128:(j + 1) * 128, :]).then_inc(s_ld, 16)
        g.dma_start(out=ocol[:, 0:1], in_=onescol[0:128, :]).then_inc(s_ld, 16)
        for tt in range(NT):
            t0 = tt * 512
            for j in range(8):
                g.wait_ge(s_dv, dv_c(tt, DC_AP(j)))
                g.dma_start(out=x1T[j * 128:(j + 1) * 128, t0:t0 + 512],
                            in_=x1s[:, j * 512:(j + 1) * 512]).then_inc(s_st, 16)
                g.dma_start(out=x1b[j * 128:(j + 1) * 128, t0:t0 + 512],
                            in_=x1bs[:, j * 512:(j + 1) * 512]).then_inc(s_st, 16)
        g.wait_ge(s_st, 16 * NT * ST_T)

    @blk.sync
    def _(s):
        s.dma_start(out=orow[:], in_=onesrow).then_inc(s_ld, 16)
        for tt in range(NT):
            t0 = tt * 512
            if tt >= 1:
                s.wait_ge(s_dv, dv_c(tt - 1, DC_XR(7)))
            for j in range(8):
                for cc in range(4):
                    s.dma_start(out=ob[:, j * 512 + cc * 128:j * 512 + cc * 128 + 128],
                                in_=oT[t0 + cc * 128:t0 + cc * 128 + 128,
                                       j * 128:(j + 1) * 128],
                                transpose=True).then_inc(s_ld, 16)
            if gate:
                for j in range(8):
                    s.dma_start(out=gb[:, j * 512:(j + 1) * 512],
                                in_=g_dt[j * 128:(j + 1) * 128, t0:t0 + 512]).then_inc(s_ld, 16)
            for j in range(8):
                s.dma_start(out=xb[:, j * 512:(j + 1) * 512],
                            in_=xT[j * 128:(j + 1) * 128, t0:t0 + 512]).then_inc(s_ld, 16)

    @blk.tensor
    def _(t):
        for tt in range(NT):
            t.wait_ge(s_ld, ld_m(tt))
            t.wait_ge(s_dv, dv_c(tt, DC_OG))
            for oc in range(8):
                pst = ps_a if oc % 2 == 0 else ps_b
                if oc >= 2:
                    t.wait_ge(s_dv, dv_c(tt, DC_XR(oc - 2)))
                elif tt > 0:
                    t.wait_ge(s_dv, dv_c(tt - 1, DC_XR(6 + oc)))
                for dj in range(8):
                    mm = t.matmul(pst[:, :], wos[:, dj * 1024 + oc * 128:dj * 1024 + oc * 128 + 128],
                                  og[:, dj * 512:(dj + 1) * 512],
                                  start=(dj == 0), stop=(dj == 7))
                    if dj == 7:
                        mm.then_inc(s_pe, 1)
            t.wait_ge(s_dv, dv_c(tt, DC_XRB))
            if tt > 0:
                t.wait_ge(s_ac, ac_c(tt - 1, 10))
            for dj in range(8):
                mm = t.matmul(pd_s[0:1, :], ocol[:, 0:1], xrb[:, dj * 512:(dj + 1) * 512],
                              start=(dj == 0), stop=(dj == 7))
                if dj == 7:
                    mm.then_inc(s_pe, 1)
            t.wait_ge(s_ac, ac_c(tt, 8))
            if tt > 0:
                t.wait_ge(s_dv, dv_c(tt - 1, DC_VAR))
            for dj in range(8):
                mm = t.matmul(pd_q[0:1, :], ocol[:, 0:1], sqb[:, dj * 512:(dj + 1) * 512],
                              start=(dj == 0), stop=(dj == 7))
                if dj == 7:
                    mm.then_inc(s_pe, 1)
            t.wait_ge(s_ac, ac_c(tt, 12))
            if tt > 0:
                t.wait_ge(s_dv, dv_c(tt - 1, DC_AP(7)))
            mm = t.matmul(ps_b1[:, :], orow[:, :], srow[0:1, 3 * 512:3 * 512 + 512],
                          start=True, stop=True)
            mm.then_inc(s_pe, 1)
            t.wait_ge(s_dv, dv_c(tt, DC_NM))
            mm = t.matmul(ps_b2[:, :], orow[:, :], srow[0:1, 4 * 512:4 * 512 + 512],
                          start=True, stop=True)
            mm.then_inc(s_pe, 1)

    @blk.vector
    def _(v):
        for tt in range(NT):
            v.wait_ge(s_ld, ld_m(tt))
            for j in range(8):
                if gate:
                    op = v.tensor_tensor(og[:, j * 512:(j + 1) * 512],
                                         ob[:, j * 512:(j + 1) * 512],
                                         gb[:, j * 512:(j + 1) * 512], ALU.mult)
                else:
                    op = v.tensor_copy(og[:, j * 512:(j + 1) * 512],
                                       ob[:, j * 512:(j + 1) * 512])
                if j == 7:
                    op.then_inc(s_dv, 8)
            for oc in range(8):
                v.wait_ge(s_pe, pe_m(tt, oc))
                pst = ps_a if oc % 2 == 0 else ps_b
                v.tensor_tensor(xr[:, oc * 512:(oc + 1) * 512], pst[:, :],
                                xb[:, oc * 512:(oc + 1) * 512], ALU.add).then_inc(s_dv, 1)
            for j in range(8):
                op = v.tensor_copy(xrb[:, j * 512:(j + 1) * 512], xr[:, j * 512:(j + 1) * 512])
                if j == 7:
                    op.then_inc(s_dv, 8)
            v.wait_ge(s_pe, pe_m(tt, 9))
            v.wait_ge(s_ac, ac_c(tt, 9))
            v.tensor_tensor(srow[0:1, 2 * 512:2 * 512 + 512], pd_q[0:1, :],
                            srow[0:1, 5 * 512:5 * 512 + 512], ALU.subtract)
            v.tensor_scalar(srow[0:1, 2 * 512:2 * 512 + 512],
                            srow[0:1, 2 * 512:2 * 512 + 512], 1e-5, None,
                            ALU.add).then_inc(s_dv, 1)
            v.wait_ge(s_ac, ac_c(tt, 12))
            v.tensor_scalar(srow[0:1, 4 * 512:4 * 512 + 512],
                            srow[0:1, 0 * 512:0 * 512 + 512], -1.0, None, ALU.mult)
            v.tensor_tensor(srow[0:1, 4 * 512:4 * 512 + 512],
                            srow[0:1, 4 * 512:4 * 512 + 512],
                            srow[0:1, 3 * 512:3 * 512 + 512], ALU.mult).then_inc(s_dv, 1)
            v.wait_ge(s_pe, pe_m(tt, 11))
            for j in range(8):
                if tt > 0:
                    v.wait_ge(s_st, st_m(tt - 1, 2 * j + 1))
                v.tensor_tensor(tmp[:, j * 512:(j + 1) * 512], ps_b1[:, :],
                                xr[:, j * 512:(j + 1) * 512], ALU.mult)
                v.tensor_tensor(x1s[:, j * 512:(j + 1) * 512],
                                tmp[:, j * 512:(j + 1) * 512], ps_b2[:, :], ALU.add)
                v.tensor_copy(x1bs[:, j * 512:(j + 1) * 512],
                              x1s[:, j * 512:(j + 1) * 512]).then_inc(s_dv, 3)

    @blk.scalar
    def _(a):
        for tt in range(NT):
            a.wait_ge(s_dv, dv_c(tt, DC_XRB))
            for j in range(8):
                op = a.activation(sqb[:, j * 512:(j + 1) * 512],
                                  xrb[:, j * 512:(j + 1) * 512], AF.Square)
                if j == 7:
                    op.then_inc(s_ac, 8)
            a.wait_ge(s_pe, pe_m(tt, 8))
            a.activation(srow[0:1, 5 * 512:5 * 512 + 512], pd_s[0:1, :],
                         AF.Square).then_inc(s_ac, 1)
            a.copy(srow[0:1, 0 * 512:0 * 512 + 512], pd_s[0:1, :]).then_inc(s_ac, 1)
            a.wait_ge(s_dv, dv_c(tt, DC_VAR))
            a.activation(srow[0:1, 1 * 512:1 * 512 + 512],
                         srow[0:1, 2 * 512:2 * 512 + 512], AF.Ln).then_inc(s_ac, 1)
            a.activation(srow[0:1, 3 * 512:3 * 512 + 512],
                         srow[0:1, 1 * 512:1 * 512 + 512], AF.Exp, scale=-0.5).then_inc(s_ac, 1)

    p.ctx.close()
    return p.nc


def build_l0_ffn(T=4096):
    p = Prog("l0_ffn")
    NT = T // 512

    x1T = p.inp("x1T", [1024, T])
    x1b = p.inp("x1b", [1024, T], BF16)
    w1T = p.inp("w1T", [1024, 4096])
    w2T = p.inp("w2T", [4096, 1024])
    b1c = p.inp("b1c", [4096, 1])
    b2c = p.inp("b2c", [1024, 1])
    scl = p.inp("scl", [128, 1])
    onescol = p.inp("onescol", [1024, 1])
    onesrow = p.inp("onesrow", [1, 128])
    ident = p.inp("ident", [128, 128])
    x2T = p.out("x2T", [1024, T])
    x2b = p.out("x2b", [1024, T], BF16)
    xrdbg = p.out("xrdbg", [1024, 512])
    srdbg = p.out("srdbg", [1, 6 * 512])

    w1s = p.sb("w1s", [128, 2 * 1024], BF16)
    w2s = p.sb("w2s", [128, 2 * 4096], BF16)
    b1s = p.sb("b1s", [128, 32], F32)
    b2s = p.sb("b2s", [128, 8], F32)
    scls = p.sb("scls", [128, 1], F32)
    ocol = p.sb("ocol", [128, 1], BF16)
    orow = p.sb("orow", [1, 128], F32)
    idents = p.sb("idents", [128, 128], F32)
    identb = p.sb("identb", [128, 128], BF16)
    lodd = p.sb("lodd", [128, 8 * 128], BF16)
    lo4 = p.sb("lo4", [128, 8 * 128], BF16)
    lev = p.sb("lev", [128, 8 * 128], BF16)
    lev4 = p.sb("lev4", [128, 8 * 128], BF16)
    e2r = p.sb("e2r", [1, 32 * 128], BF16)
    e4r = p.sb("e4r", [1, 32 * 128], BF16)
    xb = p.sb("xb", [128, 8 * 512], BF16)
    xf = p.sb("xf", [128, 8 * 512], F32)
    hs = p.sb("hs", [128, 32 * 512], BF16)
    t1r = p.sb("t1r", [1, 4 * 512], BF16)
    t2r = p.sb("t2r", [1, 16 * 512], BF16)
    xr = p.sb("xr", [128, 8 * 512], F32)
    xrb = p.sb("xrb", [128, 8 * 512], BF16)
    sqb = xb
    srow = p.sb("srow", [1, 6 * 512], F32)
    tmp = xf
    x2s = xr
    x2bs = p.sb("x2bs", [128, 8 * 512], BF16)

    ps_a = p.ps("ps_a", [128, 512])
    ps_b = p.ps("ps_b", [128, 512])
    ps_t = p.ps("ps_t", [128, 512])
    ps_i = p.ps("ps_i", [128, 4 * 512])
    ps_tb = p.ps("ps_tb", [128, 256], BF16)

    s_ld = p.sem("s_ld")
    s_pe = p.sem("s_pe")
    s_dv = p.sem("s_dv")
    s_ac = p.sem("s_ac")
    s_st = p.sem("s_st")
    s_w2 = p.sem("s_w2")
    s_w1 = p.sem("s_w1")

    blk = p.block()

    N_G = 32 + 8 + 1 + 2 + 128
    def ldg_m(n):
        return 16 * n

    PE_PREP = 16
    PE_T = 1 + 32 + 32 + 8 + 4

    def pe_m(tt, g):
        return PE_PREP + tt * PE_T + g + 1

    DV_PREP = 16 + 1
    DV_T = 1 + 4 + 8 + 8 + 1 + 1 + 24
    DC_T1 = 1
    DC_T2 = 5

    def DC_XR(oc):
        return 5 + oc + 1
    DC_XRB = 21
    DC_VAR = 22
    DC_NM = 23

    def DC_AP(j):
        return 23 + 3 * (j + 1)

    def dv_c(tt, cnt):
        return DV_PREP + tt * DV_T + cnt

    AC_T = 32 + 8 + 4

    def ac_c(tt, cnt):
        return tt * AC_T + cnt

    ST_T = 16

    def st_m(tt, i):
        return 16 * (tt * ST_T + i + 1)

    @blk.gpsimd
    def _(g):
        for j in range(32):
            g.dma_start(out=b1s[:, j:j + 1],
                        in_=b1c[j * 128:(j + 1) * 128, :]).then_inc(s_ld, 16)
        for j in range(8):
            g.dma_start(out=b2s[:, j:j + 1],
                        in_=b2c[j * 128:(j + 1) * 128, :]).then_inc(s_ld, 16)
        g.dma_start(out=scls[:], in_=scl).then_inc(s_ld, 16)
        g.dma_start(out=ocol[:, 0:1], in_=onescol[0:128, :]).then_inc(s_ld, 16)
        g.dma_start(out=identb[:], in_=ident).then_inc(s_ld, 16)
        g.wait_ge(s_dv, 1)
        for j in range(8):
            for pp in range(4):
                g.dma_start(out=lodd[:, j * 128 + 32 * pp:j * 128 + 32 * pp + 1],
                            in_=x1T[j * 128:(j + 1) * 128, 2 + 2 * pp:3 + 2 * pp],
                            allow_slow_non_contiguous=True).then_inc(s_ld, 16)
        for j in range(8):
            for pp in range(4):
                g.dma_start(out=lo4[:, j * 128 + 32 * pp:j * 128 + 32 * pp + 1],
                            in_=x1T[j * 128:(j + 1) * 128, 10 + 2 * pp:11 + 2 * pp],
                            allow_slow_non_contiguous=True).then_inc(s_ld, 16)
        for j in range(8):
            for pp in range(4):
                g.dma_start(out=lev[:, j * 128 + 32 * pp:j * 128 + 32 * pp + 1],
                            in_=x1T[j * 128:(j + 1) * 128, 1 + 2 * pp:2 + 2 * pp],
                            allow_slow_non_contiguous=True).then_inc(s_ld, 16)
        for j in range(8):
            for pp in range(4):
                g.dma_start(out=lev4[:, j * 128 + 32 * pp:j * 128 + 32 * pp + 1],
                            in_=x1T[j * 128:(j + 1) * 128, 9 + 2 * pp:10 + 2 * pp],
                            allow_slow_non_contiguous=True).then_inc(s_ld, 16)
        for tt in range(NT):
            t0 = tt * 512
            for hc in range(32):
                gh = tt * 32 + hc
                buf = gh % 2
                if gh >= 2:
                    g.wait_ge(s_ac, ac_c(tt, hc - 1) if hc >= 2 else ac_c(tt - 1, 31 + hc))
                for dj in range(8):
                    g.dma_start(out=w1s[:, buf * 1024 + dj * 128:buf * 1024 + dj * 128 + 128],
                                in_=w1T[dj * 128:(dj + 1) * 128,
                                        hc * 128:(hc + 1) * 128]).then_inc(s_w1, 16)
            for oc in range(8):
                gi = tt * 8 + oc
                buf = gi % 2
                if gi >= 2:
                    g.wait_ge(s_pe, pe_m(tt, 65 + oc - 2) if oc >= 2
                              else pe_m(tt - 1, 65 + 6 + oc))
                for j2 in range(32):
                    g.dma_start(out=w2s[:, buf * 4096 + j2 * 128:buf * 4096 + j2 * 128 + 128],
                                in_=w2T[j2 * 128:(j2 + 1) * 128,
                                        oc * 128:(oc + 1) * 128]).then_inc(s_w2, 16)
            for j in range(8):
                g.wait_ge(s_dv, dv_c(tt, DC_AP(j)))
                g.dma_start(out=x2T[j * 128:(j + 1) * 128, t0:t0 + 512],
                            in_=x2s[:, j * 512:(j + 1) * 512]).then_inc(s_st, 16)
                g.dma_start(out=x2b[j * 128:(j + 1) * 128, t0:t0 + 512],
                            in_=x2bs[:, j * 512:(j + 1) * 512]).then_inc(s_st, 16)
        g.wait_ge(s_st, 16 * NT * ST_T)
        for j in range(8):
            g.dma_start(out=xrdbg[j * 128:(j + 1) * 128, :],
                        in_=xr[:, j * 512:(j + 1) * 512]).then_inc(s_st, 16)
        g.dma_start(out=srdbg, in_=srow[0:1, 0:3072]).then_inc(s_st, 16)
        g.wait_ge(s_st, 16 * (NT * ST_T + 9))

    @blk.sync
    def _(s):
        s.dma_start(out=orow[:], in_=onesrow).then_inc(s_ld, 16)
        s.dma_start(out=idents[:], in_=ident).then_inc(s_ld, 16)
        for tt in range(NT):
            t0 = tt * 512
            if tt >= 1:
                s.wait_ge(s_dv, dv_c(tt - 1, DC_AP(7)))
                s.wait_ge(s_st, st_m(tt - 1, ST_T - 1))
            for j in range(8):
                s.dma_start(out=xb[:, j * 512:(j + 1) * 512],
                            in_=x1b[j * 128:(j + 1) * 128, t0:t0 + 512]).then_inc(s_ld, 16)
            for j in range(8):
                s.dma_start(out=xf[:, j * 512:(j + 1) * 512],
                            in_=x1T[j * 128:(j + 1) * 128, t0:t0 + 512]).then_inc(s_ld, 16)

    LD_ALL_G = N_G
    LD_CONST = LD_ALL_G + 2

    def ld_tile(tt):
        return 16 * (LD_CONST + (tt + 1) * 16)

    @blk.tensor
    def _(t):
        t.wait_ge(s_ld, 16 * LD_CONST)
        for j in range(8):
            if j > 0:
                t.wait_ge(s_dv, 1 + j)
            t.transpose(ps_tb[:, 0:128], lev[:, j * 128:(j + 1) * 128],
                        identb[:, :]).then_inc(s_pe, 1)
        for j in range(8):
            t.wait_ge(s_dv, 1 + 8 + j)
            t.transpose(ps_tb[:, 0:128], lev4[:, j * 128:(j + 1) * 128],
                        identb[:, :]).then_inc(s_pe, 1)
        for tt in range(NT):
            t.wait_ge(s_ld, ld_tile(tt))
            t.wait_ge(s_dv, DV_PREP)
            t.wait_ge(s_dv, 1 + 16)
            if tt > 0:
                t.wait_ge(s_dv, dv_c(tt - 1, DC_T1))
            for dj in range(8):
                mm = t.matmul(ps_t[:, 0:512], lodd[:, dj * 128:(dj + 1) * 128],
                              xb[:, dj * 512:(dj + 1) * 512],
                              start=(dj == 0), stop=(dj == 7))
                if dj == 7:
                    mm.then_inc(s_pe, 1)
            t.wait_ge(s_dv, dv_c(tt, DC_T1))
            for hc in range(32):
                pst = ps_a if hc % 2 == 0 else ps_b
                gh = tt * 32 + hc
                buf = gh % 2
                t.wait_ge(s_w1, 16 * 8 * (gh + 1))
                if hc >= 2:
                    t.wait_ge(s_ac, ac_c(tt, hc - 1))
                elif tt > 0:
                    t.wait_ge(s_dv, dv_c(tt - 1, DC_AP(7)))
                for dj in range(8):
                    t.matmul(pst[:, :], w1s[:, buf * 1024 + dj * 128:buf * 1024 + dj * 128 + 128],
                             xb[:, dj * 512:(dj + 1) * 512],
                             start=(dj == 0), stop=False)
                pp_ = hc // 8
                mm = t.matmul(pst[:, :],
                              e2r[0:1, (pp_ * 8 + hc % 8) * 128:(pp_ * 8 + hc % 8) * 128 + 128],
                              t1r[0:1, pp_ * 512:(pp_ + 1) * 512],
                              start=False, stop=True)
                mm.then_inc(s_pe, 1)
            if tt > 0:
                t.wait_ge(s_dv, dv_c(tt - 1, DC_T2))
            for tau in range(32):
                i2 = tau // 8
                t.wait_ge(s_ac, ac_c(tt, tau + 1))
                mm = t.matmul(ps_i[:, i2 * 512:(i2 + 1) * 512],
                              lo4[:, (tau % 8) * 128:(tau % 8) * 128 + 128],
                              hs[:, tau * 512:(tau + 1) * 512],
                              start=(tau % 8 == 0), stop=(tau % 8 == 7))
                mm.then_inc(s_pe, 1)
            t.wait_ge(s_dv, dv_c(tt, DC_T2))
            for oc in range(8):
                pst = ps_a if oc % 2 == 0 else ps_b
                if oc >= 2:
                    t.wait_ge(s_dv, dv_c(tt, DC_XR(oc - 2)))
                gi = tt * 8 + oc
                buf = gi % 2
                t.wait_ge(s_w2, 16 * 32 * (gi + 1))
                for hc in range(32):
                    t.matmul(pst[:, :], w2s[:, buf * 4096 + hc * 128:buf * 4096 + hc * 128 + 128],
                             hs[:, hc * 512:(hc + 1) * 512],
                             start=(hc == 0), stop=False)
                ppv = oc // 2
                for i2 in range(4):
                    for j4 in range(4):
                        jj = 4 * (oc % 2) + j4
                        last = (i2 == 3 and j4 == 3)
                        mm = t.matmul(pst[32 * j4:32 * j4 + 32, :],
                                      e4r[0:1, (ppv * 8 + jj) * 128 + i2:(ppv * 8 + jj + 1) * 128:4],
                                      t2r[0:1, (ppv * 4 + i2) * 512:(ppv * 4 + i2) * 512 + 512],
                                      start=False, stop=last,
                                      tile_position=(0, 32 * j4))
                        if last:
                            mm.then_inc(s_pe, 1)
            t.wait_ge(s_dv, dv_c(tt, DC_XRB))
            if tt > 0:
                t.wait_ge(s_ac, ac_c(tt - 1, 42))
            for dj in range(8):
                mm = t.matmul(ps_t[0:1, 0:512], ocol[:, 0:1], xrb[:, dj * 512:(dj + 1) * 512],
                              start=(dj == 0), stop=(dj == 7))
                if dj == 7:
                    mm.then_inc(s_pe, 1)
            t.wait_ge(s_ac, ac_c(tt, 40))
            if tt > 0:
                t.wait_ge(s_dv, dv_c(tt - 1, DC_VAR))
            for dj in range(8):
                mm = t.matmul(ps_i[0:1, 0:512], ocol[:, 0:1], sqb[:, dj * 512:(dj + 1) * 512],
                              start=(dj == 0), stop=(dj == 7))
                if dj == 7:
                    mm.then_inc(s_pe, 1)
            t.wait_ge(s_ac, ac_c(tt, 44))
            if tt > 0:
                t.wait_ge(s_dv, dv_c(tt - 1, DC_AP(7)))
            mm = t.matmul(ps_i[:, 512:1024], orow[:, :], srow[0:1, 3 * 512:3 * 512 + 512],
                          start=True, stop=True)
            mm.then_inc(s_pe, 1)
            t.wait_ge(s_dv, dv_c(tt, DC_NM))
            mm = t.matmul(ps_i[:, 1024:1536], orow[:, :], srow[0:1, 4 * 512:4 * 512 + 512],
                          start=True, stop=True)
            mm.then_inc(s_pe, 1)

    @blk.vector
    def _(v):
        v.memset(lodd[:, :], 0.0)
        v.memset(lo4[:, :], 0.0)
        v.memset(lev[:, :], 0.0)
        v.memset(lev4[:, :], 0.0)
        v.nop().then_inc(s_dv, 1)
        for j in range(8):
            v.wait_ge(s_pe, j + 1)
            for pp in range(4):
                op = v.tensor_copy(e2r[0:1, (pp * 8 + j) * 128:(pp * 8 + j) * 128 + 128],
                                   ps_tb[32 * pp:32 * pp + 1, 0:128])
                if pp == 3:
                    op.then_inc(s_dv, 1)
        for j in range(8):
            v.wait_ge(s_pe, 8 + j + 1)
            for pp in range(4):
                op = v.tensor_copy(e4r[0:1, (pp * 8 + j) * 128:(pp * 8 + j) * 128 + 128],
                                   ps_tb[32 * pp:32 * pp + 1, 0:128])
                if pp == 3:
                    op.then_inc(s_dv, 1)
        for tt in range(NT):
            v.wait_ge(s_pe, pe_m(tt, 0))
            for pp in range(4):
                op = v.tensor_scalar(t1r[0:1, pp * 512:(pp + 1) * 512],
                                     ps_t[32 * pp:32 * pp + 1, 0:512],
                                     scls[32 * pp:32 * pp + 1, 0:1], None, ALU.mult)
                if pp == 3:
                    op.then_inc(s_dv, 1)
            for i2 in range(4):
                v.wait_ge(s_pe, pe_m(tt, 33 + 8 * i2 + 7))
                for pp in range(4):
                    op = v.tensor_scalar(t2r[0:1, (pp * 4 + i2) * 512:(pp * 4 + i2) * 512 + 512],
                                         ps_i[32 * pp:32 * pp + 1, i2 * 512:(i2 + 1) * 512],
                                         scls[32 * pp:32 * pp + 1, 0:1], None, ALU.mult)
                    if i2 == 3 and pp == 3:
                        op.then_inc(s_dv, 4)
            for oc in range(8):
                v.wait_ge(s_pe, pe_m(tt, 65 + oc))
                pst = ps_a if oc % 2 == 0 else ps_b
                v.tensor_scalar(xr[:, oc * 512:(oc + 1) * 512], pst[:, :],
                                b2s[:, oc:oc + 1], None, ALU.add)
                v.tensor_tensor(xr[:, oc * 512:(oc + 1) * 512],
                                xr[:, oc * 512:(oc + 1) * 512],
                                xf[:, oc * 512:(oc + 1) * 512], ALU.add).then_inc(s_dv, 1)
            for j in range(8):
                op = v.tensor_copy(xrb[:, j * 512:(j + 1) * 512], xr[:, j * 512:(j + 1) * 512])
                if j == 7:
                    op.then_inc(s_dv, 8)
            v.wait_ge(s_pe, pe_m(tt, 74))
            v.wait_ge(s_ac, ac_c(tt, 41))
            v.tensor_tensor(srow[0:1, 2 * 512:2 * 512 + 512], ps_i[0:1, 0:512],
                            srow[0:1, 5 * 512:5 * 512 + 512], ALU.subtract)
            v.tensor_scalar(srow[0:1, 2 * 512:2 * 512 + 512],
                            srow[0:1, 2 * 512:2 * 512 + 512], 1e-5, None,
                            ALU.add).then_inc(s_dv, 1)
            v.wait_ge(s_ac, ac_c(tt, 44))
            v.tensor_scalar(srow[0:1, 4 * 512:4 * 512 + 512],
                            srow[0:1, 0 * 512:0 * 512 + 512], -1.0, None, ALU.mult)
            v.tensor_tensor(srow[0:1, 4 * 512:4 * 512 + 512],
                            srow[0:1, 4 * 512:4 * 512 + 512],
                            srow[0:1, 3 * 512:3 * 512 + 512], ALU.mult).then_inc(s_dv, 1)
            v.wait_ge(s_pe, pe_m(tt, 76))
            for j in range(8):
                if tt > 0:
                    v.wait_ge(s_st, st_m(tt - 1, 2 * j + 1))
                v.tensor_tensor(tmp[:, j * 512:(j + 1) * 512], ps_i[:, 512:1024],
                                xr[:, j * 512:(j + 1) * 512], ALU.mult)
                v.tensor_tensor(x2s[:, j * 512:(j + 1) * 512],
                                tmp[:, j * 512:(j + 1) * 512], ps_i[:, 1024:1536], ALU.add)
                v.tensor_copy(x2bs[:, j * 512:(j + 1) * 512],
                              x2s[:, j * 512:(j + 1) * 512]).then_inc(s_dv, 3)

    @blk.scalar
    def _(a):
        for tt in range(NT):
            for hc in range(32):
                a.wait_ge(s_pe, pe_m(tt, 1 + hc))
                pst = ps_a if hc % 2 == 0 else ps_b
                a.activation(hs[:, hc * 512:(hc + 1) * 512], pst[:, :],
                             AF.Gelu, bias=b1s[:, hc:hc + 1]).then_inc(s_ac, 1)
            a.wait_ge(s_dv, dv_c(tt, DC_XRB))
            for j in range(8):
                op = a.activation(sqb[:, j * 512:(j + 1) * 512],
                                  xrb[:, j * 512:(j + 1) * 512], AF.Square)
                if j == 7:
                    op.then_inc(s_ac, 8)
            a.wait_ge(s_pe, pe_m(tt, 73))
            a.activation(srow[0:1, 5 * 512:5 * 512 + 512], ps_t[0:1, 0:512],
                         AF.Square).then_inc(s_ac, 1)
            a.copy(srow[0:1, 0 * 512:0 * 512 + 512], ps_t[0:1, 0:512]).then_inc(s_ac, 1)
            a.wait_ge(s_dv, dv_c(tt, DC_VAR))
            a.activation(srow[0:1, 1 * 512:1 * 512 + 512],
                         srow[0:1, 2 * 512:2 * 512 + 512], AF.Ln).then_inc(s_ac, 1)
            a.activation(srow[0:1, 3 * 512:3 * 512 + 512],
                         srow[0:1, 1 * 512:1 * 512 + 512], AF.Exp, scale=-0.5).then_inc(s_ac, 1)

    p.ctx.close()
    return p.nc


L1P_INC = ("a", "g", "b")
L1P_BIAS = True


def build_l1_proj(T=4096):
    inc = L1P_INC
    p = Prog("l1_proj")
    NT = T // 512

    xbi = p.inp("xbi", [1024, T], BF16)
    wqkvT = p.inp("wqkvT", [1024, 2048])
    waT = p.inp("waT", [1024, 1024])
    wbT = p.inp("wbT", [1024, 8])
    wgT = p.inp("wgT", [1024, 1024])
    ba_r = p.inp("ba_r", [1, 1024])
    bb_r = p.inp("bb_r", [1, 8])
    onesrow1 = p.inp("onesrow1", [1, 128])
    qkv_dt = p.out("qkv_dt", [2048, T], BF16)
    aT_o = p.out("aT", [T, 1024])
    bT_o = p.out("bT", [T, 8])
    gT_o = p.out("gT", [T, 1024], BF16)

    wqkvs = p.sb("wqkvs", [128, 8 * 2048], BF16)
    was = p.sb("was", [128, 8 * 1024], BF16)
    wbs = p.sb("wbs", [128, 8 * 8], BF16)
    wgs = p.sb("wgs", [128, 8 * 1024], BF16)
    bar = p.sb("bar", [1, 1024], BF16)
    bbr = p.sb("bbr", [1, 8], BF16)
    or1 = p.sb("or1", [1, 128], BF16)
    xb = p.sb("xb", [128, 2 * 8 * 512], BF16)
    ev = p.sb("ev", [128, 16 * 512], BF16)
    evr = p.sb("evr", [128, 2 * 1024], F32)
    evt = p.sb("evt", [128, 2 * 1024], F32)
    evr2 = p.sb("evr2", [128, 2 * 1024], F32)
    evg = p.sb("evg", [128, 2 * 1024], BF16)
    evbr = p.sb("evbr", [128, 2 * 8], F32)
    evb = p.sb("evb", [128, 2 * 8], F32)

    ps_a = p.ps("ps_a", [128, 512])
    ps_b = p.ps("ps_b", [128, 512])

    s_ld = p.sem("s_ld")
    s_pe = p.sem("s_pe")
    s_dv = p.sem("s_dv")
    s_ac = p.sem("s_ac")
    s_st = p.sem("s_st")

    blk = p.block()
    NLD = 8 * 4 + 3

    groups = [("qkv", oc) for oc in range(16)]
    acts, stores = [], [("qkv", oc) for oc in range(16)]
    for tc in range(4):
        if "a" in inc:
            groups += [("a", tc, 0), ("a", tc, 1)]
            acts += [("a", tc, 0), ("a", tc, 1)]
            stores += [("a", tc)]
        if "g" in inc:
            groups += [("g", tc, 0), ("g", tc, 1)]
            acts += [("g", tc, 0), ("g", tc, 1)]
            stores += [("g", tc)]
        if "b" in inc:
            groups += [("b", tc)]
            acts += [("b", tc)]
            stores += [("b", tc)]
    gi = {k: n for n, k in enumerate(groups)}
    ai = {k: n for n, k in enumerate(acts)}
    si = {k: n for n, k in enumerate(stores)}
    PE_T = DV_T = len(groups)
    AC_T = len(acts)
    ST_T = len(stores)

    def pe_m(tt, g):
        return tt * PE_T + g + 1

    def dv_m(tt, i):
        return tt * DV_T + i + 1

    def ac_m(tt, i):
        return tt * AC_T + i + 1

    def st_m(tt, i):
        return 16 * (tt * ST_T + i + 1)

    @blk.gpsimd
    def _(g):
        for j in range(8):
            g.dma_start(out=wqkvs[:, j * 2048:(j + 1) * 2048],
                        in_=wqkvT[j * 128:(j + 1) * 128, :]).then_inc(s_ld, 16)
        for j in range(8):
            g.dma_start(out=was[:, j * 1024:(j + 1) * 1024],
                        in_=waT[j * 128:(j + 1) * 128, :]).then_inc(s_ld, 16)
        for j in range(8):
            g.dma_start(out=wbs[:, j * 8:(j + 1) * 8],
                        in_=wbT[j * 128:(j + 1) * 128, :]).then_inc(s_ld, 16)
        for j in range(8):
            g.dma_start(out=wgs[:, j * 1024:(j + 1) * 1024],
                        in_=wgT[j * 128:(j + 1) * 128, :]).then_inc(s_ld, 16)
        g.dma_start(out=bar[:], in_=ba_r).then_inc(s_ld, 16)
        g.dma_start(out=bbr[:], in_=bb_r).then_inc(s_ld, 16)
        g.dma_start(out=or1[:], in_=onesrow1).then_inc(s_ld, 16)
        for tt in range(NT):
            t0 = tt * 512
            for st in stores:
                if st[0] == "qkv":
                    oc = st[1]
                    g.wait_ge(s_dv, dv_m(tt, gi[st]))
                    g.dma_start(out=qkv_dt[oc * 128:(oc + 1) * 128, t0:t0 + 512],
                                in_=ev[:, oc * 512:(oc + 1) * 512]).then_inc(s_st, 16)
                elif st[0] == "a":
                    tc = st[1]
                    r0 = t0 + tc * 128
                    sl = (tc % 2) * 1024
                    g.wait_ge(s_ac, ac_m(tt, ai[("a", tc, 1)]))
                    g.dma_start(out=aT_o[r0:r0 + 128, :],
                                in_=evt[:, sl:sl + 1024]).then_inc(s_st, 16)
                elif st[0] == "g":
                    tc = st[1]
                    r0 = t0 + tc * 128
                    sl = (tc % 2) * 1024
                    g.wait_ge(s_ac, ac_m(tt, ai[("g", tc, 1)]))
                    g.dma_start(out=gT_o[r0:r0 + 128, :],
                                in_=evg[:, sl:sl + 1024]).then_inc(s_st, 16)
                else:
                    tc = st[1]
                    r0 = t0 + tc * 128
                    sl = (tc % 2) * 8
                    g.wait_ge(s_ac, ac_m(tt, ai[("b", tc)]))
                    g.dma_start(out=bT_o[r0:r0 + 128, :],
                                in_=evb[:, sl:sl + 8]).then_inc(s_st, 16)
        g.wait_ge(s_st, 16 * NT * ST_T)

    @blk.sync
    def _(s):
        for tt in range(NT):
            t0 = tt * 512
            if tt >= 2:
                s.wait_ge(s_pe, pe_m(tt - 2, PE_T - 1))
            for j in range(8):
                s.dma_start(out=xb[:, (tt % 2) * 4096 + j * 512:(tt % 2) * 4096 + j * 512 + 512],
                            in_=xbi[j * 128:(j + 1) * 128, t0:t0 + 512]).then_inc(s_ld, 16)

    def xsl(tt, dj, lo, n):
        base = (tt % 2) * 4096 + dj * 512 + lo
        return xb[:, base:base + n]

    @blk.tensor
    def _(t):
        for tt in range(NT):
            t.wait_ge(s_ld, 16 * (NLD + 8 * (tt + 1)))
            for idx, gr in enumerate(groups):
                pst = ps_a if idx % 2 == 0 else ps_b
                if idx >= 2:
                    t.wait_ge(s_dv, dv_m(tt, idx - 2))
                elif tt > 0:
                    t.wait_ge(s_dv, dv_m(tt - 1, DV_T - 2 + idx))
                if gr[0] == "qkv":
                    oc = gr[1]
                    for dj in range(8):
                        mm = t.matmul(pst[:, :],
                                      wqkvs[:, dj * 2048 + oc * 128:dj * 2048 + oc * 128 + 128],
                                      xsl(tt, dj, 0, 512), start=(dj == 0), stop=(dj == 7))
                        if dj == 7:
                            mm.then_inc(s_pe, 1)
                elif gr[0] in ("a", "g"):
                    tc, half = gr[1], gr[2]
                    ws = was if gr[0] == "a" else wgs
                    with_bias = L1P_BIAS and gr[0] == "a"
                    for dj in range(8):
                        last = (dj == 7) and not with_bias
                        mm = t.matmul(pst[:, :], xsl(tt, dj, tc * 128, 128),
                                      ws[:, dj * 1024 + half * 512:dj * 1024 + half * 512 + 512],
                                      start=(dj == 0), stop=last)
                        if last:
                            mm.then_inc(s_pe, 1)
                    if with_bias:
                        mm = t.matmul(pst[:, :], or1[0:1, 0:128],
                                      bar[0:1, half * 512:half * 512 + 512],
                                      start=False, stop=True)
                        mm.then_inc(s_pe, 1)
                else:
                    tc = gr[1]
                    for dj in range(8):
                        last = (dj == 7) and not L1P_BIAS
                        mm = t.matmul(pst[0:128, 0:8], xsl(tt, dj, tc * 128, 128),
                                      wbs[:, dj * 8:(dj + 1) * 8],
                                      start=(dj == 0), stop=last)
                        if last:
                            mm.then_inc(s_pe, 1)
                    if L1P_BIAS:
                        mm = t.matmul(pst[0:128, 0:8], or1[0:1, 0:128], bbr[0:1, 0:8],
                                      start=False, stop=True)
                        mm.then_inc(s_pe, 1)

    @blk.vector
    def _(v):
        for tt in range(NT):
            for idx, gr in enumerate(groups):
                v.wait_ge(s_pe, pe_m(tt, idx))
                pst = ps_a if idx % 2 == 0 else ps_b
                if gr[0] == "qkv":
                    oc = gr[1]
                    if tt > 0:
                        v.wait_ge(s_st, st_m(tt - 1, si[gr]))
                    v.tensor_copy(ev[:, oc * 512:(oc + 1) * 512], pst[:, :]).then_inc(s_dv, 1)
                elif gr[0] == "a":
                    tc, half = gr[1], gr[2]
                    if tc >= 2:
                        v.wait_ge(s_ac, ac_m(tt, ai[("a", tc - 2, half)]))
                    elif tt > 0:
                        v.wait_ge(s_ac, ac_m(tt - 1, ai[("a", tc + 2, half)]))
                    sl = (tc % 2) * 1024 + half * 512
                    v.tensor_copy(evr[:, sl:sl + 512], pst[:, :]).then_inc(s_dv, 1)
                elif gr[0] == "g":
                    tc, half = gr[1], gr[2]
                    if tc >= 2:
                        v.wait_ge(s_ac, ac_m(tt, ai[("g", tc - 2, half)]))
                    elif tt > 0:
                        v.wait_ge(s_ac, ac_m(tt - 1, ai[("g", tc + 2, half)]))
                    sl = (tc % 2) * 1024 + half * 512
                    v.tensor_copy(evr2[:, sl:sl + 512], pst[:, :]).then_inc(s_dv, 1)
                else:
                    tc = gr[1]
                    if tc >= 2:
                        v.wait_ge(s_ac, ac_m(tt, ai[("b", tc - 2)]))
                    elif tt > 0:
                        v.wait_ge(s_ac, ac_m(tt - 1, ai[("b", tc + 2)]))
                    sl = (tc % 2) * 8
                    v.tensor_copy(evbr[:, sl:sl + 8], pst[0:128, 0:8]).then_inc(s_dv, 1)

    if acts:
        @blk.scalar
        def _(a):
            for tt in range(NT):
                for ak in acts:
                    if ak[0] == "a":
                        tc, half = ak[1], ak[2]
                        a.wait_ge(s_dv, dv_m(tt, gi[ak]))
                        if tc >= 2:
                            a.wait_ge(s_st, st_m(tt, si[("a", tc - 2)]))
                        elif tt > 0:
                            a.wait_ge(s_st, st_m(tt - 1, si[("a", tc + 2)]))
                        sl = (tc % 2) * 1024 + half * 512
                        a.activation(evt[:, sl:sl + 512], evr[:, sl:sl + 512],
                                     AF.Sigmoid).then_inc(s_ac, 1)
                    elif ak[0] == "g":
                        tc, half = ak[1], ak[2]
                        a.wait_ge(s_dv, dv_m(tt, gi[ak]))
                        if tc >= 2:
                            a.wait_ge(s_st, st_m(tt, si[("g", tc - 2)]))
                        elif tt > 0:
                            a.wait_ge(s_st, st_m(tt - 1, si[("g", tc + 2)]))
                        sl = (tc % 2) * 1024 + half * 512
                        a.activation(evg[:, sl:sl + 512], evr2[:, sl:sl + 512],
                                     AF.Sigmoid).then_inc(s_ac, 1)
                    else:
                        tc = ak[1]
                        a.wait_ge(s_dv, dv_m(tt, gi[ak]))
                        if tc >= 2:
                            a.wait_ge(s_st, st_m(tt, si[("b", tc - 2)]))
                        elif tt > 0:
                            a.wait_ge(s_st, st_m(tt - 1, si[("b", tc + 2)]))
                        sl = (tc % 2) * 8
                        a.activation(evb[:, sl:sl + 8], evbr[:, sl:sl + 8],
                                     AF.Sigmoid).then_inc(s_ac, 1)

    p.ctx.close()
    return p.nc
def build_l1_conv(T=4096):
    p = Prog("l1_conv")
    NT = T // 512
    NR2 = T // 128

    qkv_dt = p.inp("qkv_dt", [2048, T], BF16)
    aT = p.inp("aT", [T, 1024])
    bT = p.inp("bT", [T, 8])
    wcp = p.inp("wcp", [128, 64])
    bcp = p.inp("bcp", [128, 16])
    cumc = p.inp("cumc", [128, 128])
    identb = p.inp("identb", [128, 128], BF16)
    q_dt = p.out("q_dt", [512, T], BF16)
    k_dt = p.out("k_dt", [512, T], BF16)
    kt_td = p.out("kt_td", [T, 512], BF16)
    vt_td = p.out("vt_td", [T, 1024], BF16)
    gamT = p.out("gamT", [T, 1024])
    invgT = p.out("invgT", [T, 1024])
    np2T = p.out("np2T", [T, 1024])

    wcs = p.sb("wcs", [128, 64], F32)
    bcs = p.sb("bcs", [128, 16], F32)
    cums = p.sb("cums", [128, 128], F32)
    idb = p.sb("idb", [128, 128], BF16)
    xc = p.sb("xc", [128, 2 * 515], F32)
    tm1 = p.sb("tm1", [128, 512], F32)
    tm2 = p.sb("tm2", [128, 512], F32)
    yf = p.sb("yf", [128, 512], F32)
    yb = p.sb("yb", [128, 2 * 512], BF16)
    tsb = p.sb("tsb", [128, 2 * 512], BF16)
    at2 = p.sb("at2", [128, 2 * 1024], F32)
    bt2 = p.sb("bt2", [128, 2 * 8], F32)
    lgs = p.sb("lgs", [128, 2 * 1024], F32)
    gsb = p.sb("gsb", [128, 1024], F32)
    isb = p.sb("isb", [128, 1024], F32)
    rec = p.sb("rec", [128, 1024], F32)
    nb = p.sb("nb", [128, 8], F32)
    np2 = p.sb("np2", [128, 1024], F32)

    ps_t1 = p.ps("ps_t1", [128, 512])
    ps_t2 = p.ps("ps_t2", [128, 512])
    ps_c1 = p.ps("ps_c1", [128, 512])
    ps_c2 = p.ps("ps_c2", [128, 512])

    s_ld = p.sem("s_ld")
    s_pe = p.sem("s_pe")
    s_dv = p.sem("s_dv")
    s_ac = p.sem("s_ac")
    s_st = p.sem("s_st")

    blk = p.block()
    NC0 = 4

    def kind(rt):
        return "q" if rt < 4 else ("k" if rt < 8 else "v")

    nstore = {"q": 1, "k": 5, "v": 4}
    stc = [0]
    for rt in range(16):
        for tt in range(NT):
            stc.append(stc[-1] + nstore[kind(rt)])
    S_CV = stc[-1]

    def dv_c(ci, e):
        return ci * 3 + e + 1

    def ac_c(ci):
        return ci + 1

    def pe_c(ci):
        return ci + 1

    def dv_g(gi):
        return 16 * NT * 3 + gi + 1

    def ac_g(gi, e):
        return 16 * NT + gi * 3 + e + 1

    def pe_g(gi, h):
        return 16 * NT + gi * 2 + h + 1

    def st_c(ci):
        return 16 * stc[ci + 1]

    def st_g(gi):
        return 16 * (S_CV + (gi + 1) * 3)

    @blk.gpsimd
    def _(g):
        g.dma_start(out=wcs[:], in_=wcp).then_inc(s_ld, 16)
        g.dma_start(out=bcs[:], in_=bcp).then_inc(s_ld, 16)
        g.dma_start(out=cums[:], in_=cumc).then_inc(s_ld, 16)
        g.dma_start(out=idb[:], in_=identb).then_inc(s_ld, 16)
        for rt in range(16):
            for tt in range(NT):
                ci = rt * NT + tt
                t0 = tt * 512
                par = ci % 2
                if ci >= 2:
                    g.wait_ge(s_dv, dv_c(ci - 2, 0))
                if t0 == 0:
                    g.dma_start(out=xc[:, par * 515 + 3:par * 515 + 515],
                                in_=qkv_dt[rt * 128:(rt + 1) * 128, 0:512]).then_inc(s_ld, 16)
                else:
                    g.dma_start(out=xc[:, par * 515:par * 515 + 515],
                                in_=qkv_dt[rt * 128:(rt + 1) * 128, t0 - 3:t0 + 512]).then_inc(s_ld, 16)
        for gi in range(NR2):
            par = gi % 2
            if gi >= 2:
                g.wait_ge(s_dv, dv_g(gi - 2))
                g.wait_ge(s_ac, ac_g(gi - 2, 1))
            g.dma_start(out=at2[:, par * 1024:par * 1024 + 1024],
                        in_=aT[gi * 128:(gi + 1) * 128, :]).then_inc(s_ld, 16)
            g.dma_start(out=bt2[:, par * 8:par * 8 + 8],
                        in_=bT[gi * 128:(gi + 1) * 128, :]).then_inc(s_ld, 16)

    @blk.vector
    def _(v):
        for rt in range(16):
            kd = kind(rt)
            for tt in range(NT):
                ci = rt * NT + tt
                par = ci % 2
                xb0 = par * 515
                v.wait_ge(s_ld, 16 * (NC0 + ci + 1))
                if tt == 0:
                    v.memset(xc[:, xb0:xb0 + 3], 0.0)
                v.tensor_scalar(tm1[:, :], xc[:, xb0:xb0 + 512],
                                wcs[:, rt * 4:rt * 4 + 1], None, ALU.mult)
                v.scalar_tensor_tensor(tm2[:, :], xc[:, xb0 + 1:xb0 + 513],
                                       wcs[:, rt * 4 + 1:rt * 4 + 2], tm1[:, :],
                                       ALU.mult, ALU.add)
                v.scalar_tensor_tensor(tm1[:, :], xc[:, xb0 + 2:xb0 + 514],
                                       wcs[:, rt * 4 + 2:rt * 4 + 3], tm2[:, :],
                                       ALU.mult, ALU.add)
                v.scalar_tensor_tensor(tm2[:, :], xc[:, xb0 + 3:xb0 + 515],
                                       wcs[:, rt * 4 + 3:rt * 4 + 4], tm1[:, :],
                                       ALU.mult, ALU.add).then_inc(s_dv, 1)
                v.wait_ge(s_ac, ac_c(ci))
                if ci >= 2:
                    v.wait_ge(s_st, st_c(ci - 2))
                    if kd != "q":
                        v.wait_ge(s_pe, pe_c(ci - 2))
                if kd == "k":
                    v.tensor_scalar(yb[:, par * 512:par * 512 + 512], yf[:, :],
                                    0.125, None, ALU.mult).then_inc(s_dv, 1)
                else:
                    v.tensor_copy(yb[:, par * 512:par * 512 + 512],
                                  yf[:, :]).then_inc(s_dv, 1)
                if kd != "q":
                    v.wait_ge(s_pe, pe_c(ci))
                    if ci >= 2:
                        v.wait_ge(s_st, st_c(ci - 2))
                    pst = ps_t1 if par == 0 else ps_t2
                    v.tensor_copy(tsb[:, par * 512:par * 512 + 512],
                                  pst[:, :]).then_inc(s_dv, 1)
                else:
                    v.nop().then_inc(s_dv, 1)
        for gi in range(NR2):
            par = gi % 2
            v.wait_ge(s_ld, 16 * (NC0 + 16 * NT + 2 * (gi + 1)))
            if gi >= 1:
                v.wait_ge(s_st, st_g(gi - 1))
            v.tensor_scalar(nb[:, :], bt2[:, par * 8:par * 8 + 8], -1.0, None, ALU.mult)
            v.reciprocal(rec[:, 0:512], at2[:, par * 1024:par * 1024 + 512])
            v.reciprocal(rec[:, 512:1024], at2[:, par * 1024 + 512:par * 1024 + 1024])
            for h in range(8):
                op = v.tensor_scalar(np2[:, h * 128:(h + 1) * 128],
                                     rec[:, h * 128:(h + 1) * 128],
                                     nb[:, h:h + 1], None, ALU.mult)
                if h == 7:
                    op.then_inc(s_dv, 1)

    @blk.scalar
    def _(a):
        for rt in range(16):
            for tt in range(NT):
                ci = rt * NT + tt
                a.wait_ge(s_dv, dv_c(ci, 0))
                if ci >= 1:
                    a.wait_ge(s_dv, dv_c(ci - 1, 1))
                a.activation(yf[:, :], tm2[:, :], AF.Silu,
                             bias=bcs[:, rt:rt + 1], scale=1.0).then_inc(s_ac, 1)
        for gi in range(NR2):
            par = gi % 2
            a.wait_ge(s_ld, 16 * (NC0 + 16 * NT + 2 * (gi + 1)))
            if gi >= 2:
                a.wait_ge(s_pe, pe_g(gi - 1, 1))
            a.activation(lgs[:, par * 1024:par * 1024 + 512],
                         at2[:, par * 1024:par * 1024 + 512], AF.Ln).then_inc(s_ac, 1)
            a.activation(lgs[:, par * 1024 + 512:par * 1024 + 1024],
                         at2[:, par * 1024 + 512:par * 1024 + 1024], AF.Ln).then_inc(s_ac, 1)
            if gi >= 1:
                a.wait_ge(s_st, st_g(gi - 1))
            a.wait_ge(s_pe, pe_g(gi, 0))
            a.activation(gsb[:, 0:512], ps_c1[:, :], AF.Exp)
            a.activation(isb[:, 0:512], ps_c1[:, :], AF.Exp, scale=-1.0)
            a.wait_ge(s_pe, pe_g(gi, 1))
            a.activation(gsb[:, 512:1024], ps_c2[:, :], AF.Exp)
            a.activation(isb[:, 512:1024], ps_c2[:, :], AF.Exp,
                         scale=-1.0).then_inc(s_ac, 1)

    @blk.tensor
    def _(t):
        for rt in range(16):
            kd = kind(rt)
            for tt in range(NT):
                ci = rt * NT + tt
                par = ci % 2
                if kd == "q":
                    t.nop().then_inc(s_pe, 1)
                    continue
                t.wait_ge(s_dv, dv_c(ci, 1))
                if ci >= 2:
                    t.wait_ge(s_dv, dv_c(ci - 2, 2))
                pst = ps_t1 if par == 0 else ps_t2
                for sub in range(4):
                    mm = t.matmul(pst[:, sub * 128:sub * 128 + 128],
                                  yb[:, par * 512 + sub * 128:par * 512 + sub * 128 + 128],
                                  idb[:, :], start=True, stop=True)
                    if sub == 3:
                        mm.then_inc(s_pe, 1)
        for gi in range(NR2):
            par = gi % 2
            t.wait_ge(s_ac, ac_g(gi, 0))
            if gi >= 1:
                t.wait_ge(s_ac, ac_g(gi - 1, 2))
            mm = t.matmul(ps_c1[:, :], cums[:, :],
                          lgs[:, par * 1024:par * 1024 + 512], start=True, stop=True)
            mm.then_inc(s_pe, 1)
            t.wait_ge(s_ac, ac_g(gi, 1))
            mm = t.matmul(ps_c2[:, :], cums[:, :],
                          lgs[:, par * 1024 + 512:par * 1024 + 1024], start=True, stop=True)
            mm.then_inc(s_pe, 1)

    @blk.sync
    def _(s):
        for rt in range(16):
            kd = kind(rt)
            for tt in range(NT):
                ci = rt * NT + tt
                t0 = tt * 512
                par = ci % 2
                if kd == "q":
                    s.wait_ge(s_dv, dv_c(ci, 1))
                    s.dma_start(out=q_dt[rt * 128:(rt + 1) * 128, t0:t0 + 512],
                                in_=yb[:, par * 512:par * 512 + 512]).then_inc(s_st, 16)
                elif kd == "k":
                    s.wait_ge(s_dv, dv_c(ci, 1))
                    s.dma_start(out=k_dt[(rt - 4) * 128:(rt - 3) * 128, t0:t0 + 512],
                                in_=yb[:, par * 512:par * 512 + 512]).then_inc(s_st, 16)
                    s.wait_ge(s_dv, dv_c(ci, 2))
                    for sub in range(4):
                        s.dma_start(out=kt_td[t0 + sub * 128:t0 + sub * 128 + 128,
                                              (rt - 4) * 128:(rt - 3) * 128],
                                    in_=tsb[:, par * 512 + sub * 128:par * 512 + sub * 128 + 128]
                                    ).then_inc(s_st, 16)
                else:
                    s.wait_ge(s_dv, dv_c(ci, 2))
                    for sub in range(4):
                        s.dma_start(out=vt_td[t0 + sub * 128:t0 + sub * 128 + 128,
                                              (rt - 8) * 128:(rt - 7) * 128],
                                    in_=tsb[:, par * 512 + sub * 128:par * 512 + sub * 128 + 128]
                                    ).then_inc(s_st, 16)
        for gi in range(NR2):
            r0 = gi * 128
            s.wait_ge(s_ac, ac_g(gi, 2))
            s.dma_start(out=gamT[r0:r0 + 128, :], in_=gsb[:, :]).then_inc(s_st, 16)
            s.dma_start(out=invgT[r0:r0 + 128, :], in_=isb[:, :]).then_inc(s_st, 16)
            s.wait_ge(s_dv, dv_g(gi))
            s.dma_start(out=np2T[r0:r0 + 128, :], in_=np2[:, :]).then_inc(s_st, 16)
        s.wait_ge(s_st, 16 * (S_CV + NR2 * 3))

    p.ctx.close()
    return p.nc
def build_l1_scan(T=4096, C=32):
    p = Prog("l1_scan")
    NCH = T // C
    SC = 256
    CPS = SC // C
    NSC = T // SC

    q_dt = p.inp("q_dt", [512, T], BF16)
    k_dt = p.inp("k_dt", [512, T], BF16)
    kt_td = p.inp("kt_td", [T, 512], BF16)
    vt_td = p.inp("vt_td", [T, 1024], BF16)
    gamT = p.inp("gamT", [T, 1024])
    invgT = p.inp("invgT", [T, 1024])
    np2T = p.inp("np2T", [T, 1024])
    bT = p.inp("bT", [T, 8])
    gT = p.inp("gT", [T, 1024], BF16)
    mkS = p.inp("mkS", [128, 128])
    mkI = p.inp("mkI", [128, 128])
    selz = p.inp("selz", [8, 512])
    lnwx = p.inp("lnwx", [128, 128])
    lnbx = p.inp("lnbx", [128, 128])
    oT = p.out("oT", [T, 1024], BF16)

    kb = p.sb("kb", [128, 4 * 2 * SC], BF16)
    qb = p.sb("qb", [128, 4 * 2 * SC], BF16)
    fbT = p.sb("fbT", [128, 2 * 2 * 8], F32)
    gm = p.sb("gm", [128, 2 * 2 * 128], F32)
    im = p.sb("im", [128, 2 * 2 * 128], F32)
    pm = p.sb("pm", [128, 2 * 2 * 128], F32)
    vm = p.sb("vm", [128, 2 * 2 * 128], BF16)
    gxm = p.sb("gxm", [128, 2 * 2 * 128], BF16)
    kzp = p.sb("kzp", [128, 2 * 4 * 128], BF16)
    gcr = p.sb("gcr", [8, 2 * 128], F32)
    mkSs = p.sb("mkSs", [128, 128], F32)
    mkIs = p.sb("mkIs", [128, 128], F32)
    selzs = p.sb("selzs", [8, 512], F32)
    lnws = p.sb("lnws", [128, 128], F32)
    lnbs = p.sb("lnbs", [128, 128], F32)
    mblk = p.sb("mblk", [128, 256], F32)
    qkm = p.sb("qkm", [128, 256], F32)
    rb = p.sb("rb", [128, 256], F32)
    xa = p.sb("xa", [128, 256], F32)
    xb2 = p.sb("xb2", [128, 256], F32)
    xmb = p.sb("xmb", [128, 256], BF16)
    vmf = p.sb("vmf", [128, 256], F32)
    gxf = p.sb("gxf", [128, 256], F32)
    bcolx = p.sb("bcolx", [128, 2 * 2], F32)
    zs = p.sb("zs", [128, 512], F32)
    zb = p.sb("zb", [128, 512], BF16)
    gnt = p.sb("gnt", [128, 256], F32)
    gnu = p.sb("gnu", [128, 256], F32)
    st8 = p.sb("st8", [128, 16], F32)
    epsc = p.sb("epsc", [128, 1], F32)
    gnv = p.sb("gnv", [128, 256], F32)
    osb = p.sb("osb", [128, 2 * 256], BF16)

    ps_m = p.ps("ps_m", [128, 256])
    ps_q = p.ps("ps_q", [128, 256])
    ps_gow = p.ps("ps_gow", [128, 512])
    ps_go = p.ps("ps_go", [128, 256])
    ps_h1 = p.ps("ps_h1", [128, 256])
    ps_h2 = p.ps("ps_h2", [128, 256])
    ps_z = p.ps("ps_z", [128, 512])
    ps_zc = p.ps("ps_zc", [128, 512])

    s_ld = p.sem("s_ld")
    s_l2 = p.sem("s_l2")
    s_pe = p.sem("s_pe")
    s_dv = p.sem("s_dv")
    s_ac = p.sem("s_ac")
    s_st = p.sem("s_st")

    blk = p.block()
    NIT = C - 1
    PE_T = 36
    DV_T = 41
    DV0 = 1
    AC_T = 4
    NCONST = 5
    LD_CH = 41
    L2_SC = 10
    L2_CH = 8

    def pe_m(c, e):
        return c * PE_T + e

    def dv_m(c, e):
        return DV0 + c * DV_T + e

    def ac_m(c, e):
        return c * AC_T + e

    def ld_m(c):
        return 16 * (NCONST + (c + 1) * LD_CH)

    def l2_m(c):
        sc = c // CPS
        return 16 * ((sc + 1) * L2_SC + (c + 1) * L2_CH)

    def st_m(c):
        return 16 * (c + 1) * 8

    @blk.sync
    def _(s):
        s.dma_start(out=mkSs[:], in_=mkS).then_inc(s_ld, 16)
        s.dma_start(out=mkIs[:], in_=mkI).then_inc(s_ld, 16)
        s.dma_start(out=selzs[:], in_=selz).then_inc(s_ld, 16)
        s.dma_start(out=lnws[:], in_=lnwx).then_inc(s_ld, 16)
        s.dma_start(out=lnbs[:], in_=lnbx).then_inc(s_ld, 16)
        for c in range(NCH):
            t0 = c * C
            par = c % 2
            if c >= 2:
                s.wait_ge(s_dv, dv_m(c - 2, 40))
                s.wait_ge(s_pe, pe_m(c - 2, 36))
            for g in range(2):
                sl = (par * 2 + g) * 128
                for src, dst in ((gamT, gm), (invgT, im), (np2T, pm),
                                 (vt_td, vm), (gT, gxm)):
                    for i in range(4):
                        s.dma_start(out=dst[32 * i:32 * i + 32, sl:sl + 128],
                                    in_=src[t0:t0 + C, (4 * g + i) * 128:(4 * g + i + 1) * 128]
                                    ).then_inc(s_ld, 16)
            s.dma_start(out=gcr[:, par * 128:par * 128 + 128],
                        in_=gamT[t0 + C - 1:t0 + C, :]
                        .rearrange("t (b c) -> (t b) c", b=8)).then_inc(s_ld, 16)

    @blk.gpsimd
    def _(g):
        for c in range(NCH):
            t0 = c * C
            sc, ci = divmod(c, CPS)
            par = c % 2
            spar = sc % 2
            if ci == 0:
                if sc >= 2:
                    g.wait_ge(s_pe, pe_m((sc - 1) * CPS - 1, 34))
                    g.wait_ge(s_dv, dv_m((sc - 1) * CPS - 1, 2))
                for j in range(4):
                    g.dma_start(out=kb[:, (j * 2 + spar) * SC:(j * 2 + spar) * SC + SC],
                                in_=k_dt[j * 128:(j + 1) * 128, sc * SC:(sc + 1) * SC]
                                ).then_inc(s_l2, 16)
                for j in range(4):
                    g.dma_start(out=qb[:, (j * 2 + spar) * SC:(j * 2 + spar) * SC + SC],
                                in_=q_dt[j * 128:(j + 1) * 128, sc * SC:(sc + 1) * SC]
                                ).then_inc(s_l2, 16)
                for hf in range(2):
                    g.dma_start(out=fbT[:, (spar * 2 + hf) * 8:(spar * 2 + hf) * 8 + 8],
                                in_=bT[sc * SC + hf * 128:sc * SC + hf * 128 + 128, :]
                                ).then_inc(s_l2, 16)
            if c >= 2:
                g.wait_ge(s_pe, pe_m(c - 2, 35))
            if c < 2:
                g.wait_ge(s_dv, 1)
            for gg in range(2):
                for jp in range(2):
                    base = par * 512 + (gg * 2 + jp) * 128
                    hA = 4 * gg + 2 * jp
                    g.dma_start(out=kzp[64 * jp:64 * jp + 32, base:base + 64],
                                in_=kt_td[t0:t0 + C, hA * 64:hA * 64 + 64]).then_inc(s_l2, 16)
                    g.dma_start(out=kzp[64 * jp + 32:64 * jp + 64, base + 64:base + 128],
                                in_=kt_td[t0:t0 + C, hA * 64 + 64:hA * 64 + 128]).then_inc(s_l2, 16)
            g.wait_ge(s_dv, dv_m(c, 40))
            for gg in range(2):
                for i in range(4):
                    g.dma_start(out=oT[t0:t0 + C, (4 * gg + i) * 128:(4 * gg + i + 1) * 128],
                                in_=osb[32 * i:32 * i + 32, par * 256 + gg * 128:par * 256 + gg * 128 + 128]
                                ).then_inc(s_st, 16)
        g.wait_ge(s_st, st_m(NCH - 1))

    def kch(c, h, q=False):
        sc, ci = divmod(c, CPS)
        spar = sc % 2
        j = h // 2
        base = (j * 2 + spar) * SC + ci * C
        buf = qb if q else kb
        return buf[64 * (h % 2):64 * (h % 2) + 64, base:base + C]

    @blk.tensor
    def _(t):
        for c in range(NCH):
            par = c % 2
            t.wait_ge(s_l2, l2_m(c))
            if c > 0:
                t.wait_ge(s_dv, dv_m(c - 1, 1))
            for h in range(8):
                gg, i = divmod(h, 4)
                mm = t.matmul(ps_m[32 * i:32 * i + 32, gg * 128 + 32 * i:gg * 128 + 32 * i + 32],
                              kch(c, h), kch(c, h), start=True, stop=True,
                              tile_position=(64 * (h % 2), 32 * i))
            for h in range(8):
                gg, i = divmod(h, 4)
                mm = t.matmul(ps_q[32 * i:32 * i + 32, gg * 128 + 32 * i:gg * 128 + 32 * i + 32],
                              kch(c, h), kch(c, h, q=True), start=True, stop=True,
                              tile_position=(64 * (h % 2), 32 * i))
                if h == 7:
                    mm.then_inc(s_pe, 1)
            if c > 0:
                t.wait_ge(s_dv, dv_m(c - 1, 41))
            else:
                t.wait_ge(s_dv, 1)
            for h in range(8):
                gg, i = divmod(h, 4)
                mm = t.matmul(ps_gow[32 * i:32 * i + 32, gg * 128:gg * 128 + 128],
                              kch(c, h), zb[64 * (h % 2):64 * (h % 2) + 64,
                                            (h // 2) * 128:(h // 2) * 128 + 128],
                              start=True, stop=True, tile_position=(64 * (h % 2), 32 * i))
                if h == 7:
                    mm.then_inc(s_pe, 1)
            for n in range(NIT):
                t.wait_ge(s_dv, dv_m(c, 2 + n))
                psh = ps_h1 if n % 2 == 0 else ps_h2
                mov = rb if n == 0 else (xa if (n - 1) % 2 == 0 else xb2)
                for gg in range(2):
                    mm = t.matmul(psh[:, gg * 128:gg * 128 + 128],
                                  mblk[:, gg * 128:gg * 128 + 128],
                                  mov[:, gg * 128:gg * 128 + 128], start=True, stop=True)
                    if gg == 1:
                        mm.then_inc(s_pe, 1)
            t.wait_ge(s_dv, dv_m(c, 2 + NIT))
            for h in range(8):
                gg, i = divmod(h, 4)
                t.matmul(ps_go[32 * i:32 * i + 32, gg * 128:gg * 128 + 128],
                         kch(c, h, q=True), zb[64 * (h % 2):64 * (h % 2) + 64,
                                               (h // 2) * 128:(h // 2) * 128 + 128],
                         start=True, stop=True, tile_position=(64 * (h % 2), 32 * i))
            xfin = xa if (NIT - 1) % 2 == 0 else xb2
            for gg in range(2):
                mm = t.matmul(ps_gow[:, 256 + gg * 128:256 + gg * 128 + 128],
                              qkm[:, gg * 128:gg * 128 + 128],
                              xfin[:, gg * 128:gg * 128 + 128], start=True, stop=True)
                if gg == 1:
                    mm.then_inc(s_pe, 1)
            t.wait_ge(s_dv, dv_m(c, 34))
            if c > 0:
                t.wait_ge(s_dv, dv_m(c - 1, 41))
            for gg in range(2):
                for jp in range(2):
                    base = par * 512 + (gg * 2 + jp) * 128
                    mm = t.matmul(ps_z[:, (2 * gg + jp) * 128:(2 * gg + jp) * 128 + 128],
                                  kzp[:, base:base + 128],
                                  xmb[:, gg * 128:gg * 128 + 128], start=True, stop=True)
                    if gg == 1 and jp == 1:
                        mm.then_inc(s_pe, 1)
            t.wait_ge(s_ld, ld_m(c))
            for qq in range(4):
                mm = t.matmul(ps_zc[:, qq * 128:qq * 128 + 128],
                              selzs[:, qq * 128:qq * 128 + 128],
                              gcr[0:8, par * 128:par * 128 + 128], start=True, stop=True)
                if qq == 3:
                    mm.then_inc(s_pe, 1)

    @blk.vector
    def _(v):
        v.memset(zs[:, :], 0.0)
        v.memset(zb[:, :], 0.0)
        v.memset(kzp[:, :], 0.0)
        v.memset(epsc[:, :], 1e-5)
        v.nop().then_inc(s_dv, 1)
        for c in range(NCH):
            sc, ci = divmod(c, CPS)
            par = c % 2
            spar = sc % 2
            hf = ci // 4
            rbase = (ci % 4) * 32
            slot = (spar * 2 + hf) * 8
            v.wait_ge(s_l2, 16 * ((sc + 1) * L2_SC + c * L2_CH))
            for gg in range(2):
                for i in range(4):
                    v.tensor_copy(bcolx[32 * i:32 * i + 32, par * 2 + gg:par * 2 + gg + 1],
                                  fbT[rbase:rbase + 32, slot + 4 * gg + i:slot + 4 * gg + i + 1])
            v.wait_ge(s_pe, pe_m(c, 1))
            if c > 0:
                v.wait_ge(s_pe, pe_m(c - 1, 34))
            for gg in range(2):
                v.tensor_tensor(mblk[:, gg * 128:gg * 128 + 128],
                                ps_m[:, gg * 128:gg * 128 + 128], mkSs[:, :], ALU.mult)
            for gg in range(2):
                op = v.tensor_tensor(qkm[:, gg * 128:gg * 128 + 128],
                                     ps_q[:, gg * 128:gg * 128 + 128], mkIs[:, :], ALU.mult)
                if gg == 1:
                    op.then_inc(s_dv, 1)
            v.wait_ge(s_ld, ld_m(c))
            v.wait_ge(s_pe, pe_m(c, 2))
            for gg in range(2):
                sl = (par * 2 + gg) * 128
                v.tensor_copy(vmf[:, gg * 128:gg * 128 + 128], vm[:, sl:sl + 128])
                v.tensor_copy(gxf[:, gg * 128:gg * 128 + 128], gxm[:, sl:sl + 128])
            for gg in range(2):
                sl = (par * 2 + gg) * 128
                v.tensor_tensor(xa[:, gg * 128:gg * 128 + 128],
                                vmf[:, gg * 128:gg * 128 + 128], im[:, sl:sl + 128], ALU.mult)
                v.tensor_scalar(xb2[:, gg * 128:gg * 128 + 128],
                                xa[:, gg * 128:gg * 128 + 128],
                                bcolx[:, par * 2 + gg:par * 2 + gg + 1], None, ALU.mult)
                v.tensor_tensor(xa[:, gg * 128:gg * 128 + 128],
                                ps_gow[:, gg * 128:gg * 128 + 128], pm[:, sl:sl + 128], ALU.mult)
                op = v.tensor_tensor(rb[:, gg * 128:gg * 128 + 128],
                                     xb2[:, gg * 128:gg * 128 + 128],
                                     xa[:, gg * 128:gg * 128 + 128], ALU.add)
                if gg == 1:
                    op.then_inc(s_dv, 1)
            for n in range(NIT):
                v.wait_ge(s_pe, pe_m(c, 3 + n))
                psh = ps_h1 if n % 2 == 0 else ps_h2
                dst = xa if n % 2 == 0 else xb2
                for gg in range(2):
                    sl = (par * 2 + gg) * 128
                    v.tensor_tensor(gnt[:, gg * 128:gg * 128 + 128],
                                    psh[:, gg * 128:gg * 128 + 128], pm[:, sl:sl + 128],
                                    ALU.mult)
                for gg in range(2):
                    op = v.tensor_tensor(dst[:, gg * 128:gg * 128 + 128],
                                         gnt[:, gg * 128:gg * 128 + 128],
                                         rb[:, gg * 128:gg * 128 + 128], ALU.add)
                    if gg == 1:
                        op.then_inc(s_dv, 1)
            xfin = xa if (NIT - 1) % 2 == 0 else xb2
            v.tensor_copy(xmb[:, :], xfin[:, :]).then_inc(s_dv, 1)
            v.wait_ge(s_pe, pe_m(c, 34))
            for gg in range(2):
                sl = (par * 2 + gg) * 128
                v.tensor_copy(gnt[:, gg * 128:gg * 128 + 128],
                              ps_gow[:, 256 + gg * 128:256 + gg * 128 + 128])
                v.tensor_tensor(gnu[:, gg * 128:gg * 128 + 128],
                                ps_go[:, gg * 128:gg * 128 + 128],
                                gnt[:, gg * 128:gg * 128 + 128], ALU.add)
                v.tensor_tensor(gnt[:, gg * 128:gg * 128 + 128],
                                gnu[:, gg * 128:gg * 128 + 128], gm[:, sl:sl + 128],
                                ALU.mult)
                op = v.reduce_sum(st8[:, gg * 8:gg * 8 + 1], gnt[:, gg * 128:gg * 128 + 128],
                                  axis=mybir.AxisListType.X)
                op.then_inc(s_dv, 1)
            for gg in range(2):
                v.wait_ge(s_ac, ac_m(c, gg + 1))
                v.tensor_scalar(gnu[:, gg * 128:gg * 128 + 128],
                                gnt[:, gg * 128:gg * 128 + 128],
                                st8[:, gg * 8 + 2:gg * 8 + 3], None, ALU.add)
                v.tensor_tensor(gnv[:, gg * 128:gg * 128 + 128],
                                gnu[:, gg * 128:gg * 128 + 128],
                                gnu[:, gg * 128:gg * 128 + 128], ALU.mult)
                op = v.reduce_sum(st8[:, gg * 8 + 1:gg * 8 + 2],
                                  gnv[:, gg * 128:gg * 128 + 128],
                                  axis=mybir.AxisListType.X)
                op.then_inc(s_dv, 1)
            for gg in range(2):
                sl = (par * 2 + gg) * 128
                v.wait_ge(s_ac, ac_m(c, gg + 3))
                if c >= 2:
                    v.wait_ge(s_st, st_m(c - 2))
                v.tensor_scalar(gnt[:, gg * 128:gg * 128 + 128],
                                gnu[:, gg * 128:gg * 128 + 128],
                                st8[:, gg * 8 + 4:gg * 8 + 5], None, ALU.mult)
                v.tensor_tensor(gnu[:, gg * 128:gg * 128 + 128],
                                gnt[:, gg * 128:gg * 128 + 128], lnws[:, :], ALU.mult)
                v.tensor_tensor(gnt[:, gg * 128:gg * 128 + 128],
                                gnu[:, gg * 128:gg * 128 + 128], lnbs[:, :], ALU.add)
                op = v.tensor_tensor(osb[:, par * 256 + gg * 128:par * 256 + gg * 128 + 128],
                                     gnt[:, gg * 128:gg * 128 + 128],
                                     gxf[:, gg * 128:gg * 128 + 128], ALU.mult)
                op.then_inc(s_dv, 1)
            v.wait_ge(s_pe, pe_m(c, 36))
            v.tensor_tensor(gnt[:, 0:256], ps_z[:, 0:256], zs[:, 0:256], ALU.add)
            v.tensor_tensor(gnu[:, 0:256], ps_z[:, 256:512], zs[:, 256:512], ALU.add)
            v.tensor_tensor(zs[:, 0:256], ps_zc[:, 0:256], gnt[:, 0:256], ALU.mult)
            v.tensor_tensor(zs[:, 256:512], ps_zc[:, 256:512], gnu[:, 0:256], ALU.mult)
            v.tensor_copy(zb[:, :], zs[:, :]).then_inc(s_dv, 1)

    @blk.scalar
    def _(a):
        for c in range(NCH):
            for gg in range(2):
                a.wait_ge(s_dv, dv_m(c, 35 + gg))
                a.activation(st8[:, gg * 8 + 2:gg * 8 + 3], st8[:, gg * 8:gg * 8 + 1],
                             AF.Copy, scale=-1.0 / 128.0).then_inc(s_ac, 1)
            for gg in range(2):
                a.wait_ge(s_dv, dv_m(c, 37 + gg))
                a.activation(st8[:, gg * 8 + 3:gg * 8 + 4], st8[:, gg * 8 + 1:gg * 8 + 2],
                             AF.Ln, bias=epsc[:, 0:1], scale=1.0 / 128.0)
                a.activation(st8[:, gg * 8 + 4:gg * 8 + 5], st8[:, gg * 8 + 3:gg * 8 + 4],
                             AF.Exp, scale=-0.5).then_inc(s_ac, 1)

    p.ctx.close()
    return p.nc
MAGIC = 12582912.0


def build_l1_fw():
    p = Prog("l1_fw")
    w1T = p.inp("w1T", [1024, 4096])
    w2T = p.inp("w2T", [4096, 1024])
    onescolf = p.inp("onescolf", [128, 1])
    onesrowf = p.inp("onesrowf", [1, 128])
    w1qT = p.out("w1qT", [1024, 4096], BF16)
    w2qT = p.out("w2qT", [4096, 1024], BF16)
    sinv = p.out("sinv", [1, 2])

    wt = p.sb("wt", [128, 2 * 4096], F32)
    col2 = p.sb("col2", [128, 2], F32)
    ocf = p.sb("ocf", [128, 1], F32)
    orf = p.sb("orf", [1, 128], F32)
    mrow = p.sb("mrow", [1, 8], F32)
    swcol = p.sb("swcol", [128, 2], F32)
    qf1 = p.sb("qf1", [128, 4096], F32)
    qf2 = p.sb("qf2", [128, 4096], F32)
    qt = p.sb("qt", [128, 2 * 4096], BF16)
    psA = p.ps("psA", [1, 2])
    psB = p.ps("psB", [128, 2])

    s_ld = p.sem("s_ld")
    s_pe = p.sem("s_pe")
    s_dv = p.sem("s_dv")
    s_ac = p.sem("s_ac")
    s_st = p.sem("s_st")
    blk = p.block()

    WS = [(w1T, w1qT, 8, 4096), (w2T, w2qT, 32, 1024)]
    NTOT = 40

    def gidx(w, r):
        return r if w == 0 else 8 + r

    @blk.gpsimd
    def _(g):
        g.dma_start(out=ocf[:], in_=onescolf).then_inc(s_ld, 16)
        g.dma_start(out=orf[:], in_=onesrowf).then_inc(s_ld, 16)
        for w, (src, dst, nrt, nc_) in enumerate(WS):
            for r in range(nrt):
                gi = gidx(w, r)
                if gi >= 2:
                    g.wait_ge(s_dv, gi - 1)
                g.dma_start(out=wt[:, (gi % 2) * 4096:(gi % 2) * 4096 + nc_],
                            in_=src[r * 128:(r + 1) * 128, :]).then_inc(s_ld, 16)
        for w, (src, dst, nrt, nc_) in enumerate(WS):
            for r in range(nrt):
                gi = gidx(w, r)
                if gi == 0:
                    g.wait_ge(s_dv, NTOT)
                else:
                    g.wait_ge(s_dv, NTOT + 2 + gi - 1)
                g.dma_start(out=wt[:, (gi % 2) * 4096:(gi % 2) * 4096 + nc_],
                            in_=src[r * 128:(r + 1) * 128, :]).then_inc(s_ld, 16)

    @blk.vector
    def _(v):
        for w, (src, dst, nrt, nc_) in enumerate(WS):
            for r in range(nrt):
                gi = gidx(w, r)
                v.wait_ge(s_ld, 16 * (2 + gi + 1))
                v.reduce_sum(col2[:, gi % 2:gi % 2 + 1],
                             wt[:, (gi % 2) * 4096:(gi % 2) * 4096 + nc_],
                             axis=mybir.AxisListType.X,
                             apply_absolute_value=True).then_inc(s_dv, 1)
        for w in range(2):
            v.wait_ge(s_ac, w + 1)
            v.tensor_scalar(mrow[0:1, 2 + w:3 + w], mrow[0:1, w:w + 1],
                            1e-5, None, ALU.max).then_inc(s_dv, 1)
        for w, (src, dst, nrt, nc_) in enumerate(WS):
            for r in range(nrt):
                gi = gidx(w, r)
                v.wait_ge(s_ld, 16 * (2 + NTOT + gi + 1))
                v.wait_ge(s_ac, 4 + w + 1)
                if gi >= 2:
                    v.wait_ge(s_st, 16 * (1 + gi - 1))
                sl = (gi % 2) * 4096
                v.tensor_scalar(qf1[:, 0:nc_], wt[:, sl:sl + nc_],
                                swcol[:, w:w + 1], None, ALU.mult)
                v.tensor_scalar(qf2[:, 0:nc_], qf1[:, 0:nc_], MAGIC, None, ALU.add)
                v.tensor_scalar(qf1[:, 0:nc_], qf2[:, 0:nc_], -MAGIC, None, ALU.add)
                v.tensor_scalar(qf2[:, 0:nc_], qf1[:, 0:nc_], 1.0, None, ALU.min)
                v.tensor_scalar(qt[:, sl:sl + nc_], qf2[:, 0:nc_], -1.0, None,
                                ALU.max).then_inc(s_dv, 1)

    @blk.tensor
    def _(t):
        for w, (src, dst, nrt, nc_) in enumerate(WS):
            for r in range(nrt):
                gi = gidx(w, r)
                t.wait_ge(s_dv, gi + 1)
                mm = t.matmul(psA[0:1, w:w + 1], ocf[:, 0:1],
                              col2[:, gi % 2:gi % 2 + 1],
                              start=(r == 0), stop=(r == nrt - 1))
                if r == nrt - 1:
                    mm.then_inc(s_pe, 1)
        for w in range(2):
            t.wait_ge(s_ac, 2 + w + 1)
            mm = t.matmul(psB[:, w:w + 1], orf[0:1, :], mrow[0:1, 6 + w:7 + w],
                          start=True, stop=True)
            mm.then_inc(s_pe, 1)

    @blk.scalar
    def _(a):
        for w, (src, dst, nrt, nc_) in enumerate(WS):
            a.wait_ge(s_pe, w + 1)
            a.activation(mrow[0:1, w:w + 1], psA[0:1, w:w + 1], AF.Copy,
                         scale=1.0 / (nrt * 128 * nc_)).then_inc(s_ac, 1)
        for w in range(2):
            a.wait_ge(s_dv, NTOT + w + 1)
            a.activation(mrow[0:1, 4 + w:5 + w], mrow[0:1, 2 + w:3 + w], AF.Ln)
            a.activation(mrow[0:1, 6 + w:7 + w], mrow[0:1, 4 + w:5 + w],
                         AF.Exp, scale=-1.0).then_inc(s_ac, 1)
        for w in range(2):
            a.wait_ge(s_pe, 2 + w + 1)
            a.copy(swcol[:, w:w + 1], psB[:, w:w + 1]).then_inc(s_ac, 1)

    @blk.sync
    def _(s):
        s.wait_ge(s_dv, NTOT + 2)
        s.dma_start(out=sinv, in_=mrow[0:1, 2:4]).then_inc(s_st, 16)
        for w, (src, dst, nrt, nc_) in enumerate(WS):
            for r in range(nrt):
                gi = gidx(w, r)
                s.wait_ge(s_dv, NTOT + 2 + gi + 1)
                s.dma_start(out=dst[r * 128:(r + 1) * 128, :],
                            in_=qt[:, (gi % 2) * 4096:(gi % 2) * 4096 + nc_]
                            ).then_inc(s_st, 16)
        s.wait_ge(s_st, 16 * (1 + NTOT))

    p.ctx.close()
    return p.nc


def build_l1_h(T=4096):
    p = Prog("l1_h")
    NT = T // 512
    x3T = p.inp("x3T", [1024, T])
    w1qT = p.inp("w1qT", [1024, 4096], BF16)
    sinv = p.inp("sinv", [1, 2])
    onesrowf = p.inp("onesrowf", [1, 128])
    h_dt = p.out("h_dt", [4096, T], BF16)

    w1qs = p.sb("w1qs", [128, 8 * 4096], BF16)
    sinvs = p.sb("sinvs", [1, 2], F32)
    orf = p.sb("orf", [1, 128], F32)
    xb = p.sb("xb", [128, 8 * 512], F32)
    asb = p.sb("asb", [128, 8 * 512], F32)
    mx = p.sb("mx", [128, 512], F32)
    rowb = p.sb("rowb", [1, 5 * 512], F32)
    sxi = p.sb("sxi", [128, 512], F32)
    xq = p.sb("xq", [128, 8 * 512], BF16)
    qf = p.sb("qf", [128, 512], F32)
    hb = p.sb("hb", [128, 2 * 512], BF16)
    ph1 = p.ps("ph1", [128, 512])
    ph2 = p.ps("ph2", [128, 512])
    pbc1 = p.ps("pbc1", [128, 512])
    pbc2 = p.ps("pbc2", [128, 512])

    s_ld = p.sem("s_ld")
    s_pe = p.sem("s_pe")
    s_dv = p.sem("s_dv")
    s_ac = p.sem("s_ac")
    s_st = p.sem("s_st")
    s_gp = p.sem("s_gp")
    blk = p.block()
    DV_T = 3 + 32
    PE_T = 2 + 32

    def dv_m(tt, e):
        return tt * DV_T + e

    def pe_m(tt, e):
        return tt * PE_T + e

    @blk.gpsimd
    def _(g):
        for j in range(8):
            g.dma_start(out=w1qs[:, j * 4096:(j + 1) * 4096],
                        in_=w1qT[j * 128:(j + 1) * 128, :]).then_inc(s_ld, 16)
        g.dma_start(out=sinvs[:], in_=sinv).then_inc(s_ld, 16)
        g.dma_start(out=orf[:], in_=onesrowf).then_inc(s_ld, 16)
        for tt in range(NT):
            t0 = tt * 512
            if tt >= 1:
                g.wait_ge(s_dv, dv_m(tt - 1, 3))
            for j in range(8):
                g.dma_start(out=xb[:, j * 512:(j + 1) * 512],
                            in_=x3T[j * 128:(j + 1) * 128, t0:t0 + 512]).then_inc(s_ld, 16)
            g.wait_ge(s_dv, dv_m(tt, 1))
            g.reduce_max(rowb[0:1, 0:512], mx[:, :],
                         axis=mybir.AxisListType.C).then_inc(s_gp, 1)

    @blk.scalar
    def _(a):
        for tt in range(NT):
            a.wait_ge(s_ld, 16 * (10 + 8 * (tt + 1)))
            if tt >= 1:
                a.wait_ge(s_dv, dv_m(tt - 1, 1))
            for j in range(8):
                op = a.activation(asb[:, j * 512:(j + 1) * 512],
                                  xb[:, j * 512:(j + 1) * 512], AF.Abs)
                if j == 7:
                    op.then_inc(s_ac, 1)

    @blk.vector
    def _(v):
        for tt in range(NT):
            v.wait_ge(s_ac, tt + 1)
            v.tensor_tensor(mx[:, :], asb[:, 0:512], asb[:, 512:1024], ALU.max)
            for j in range(2, 7):
                v.tensor_tensor(mx[:, :], mx[:, :], asb[:, j * 512:(j + 1) * 512],
                                ALU.max)
            v.tensor_tensor(mx[:, :], mx[:, :], asb[:, 7 * 512:8 * 512],
                            ALU.max).then_inc(s_dv, 1)
            v.wait_ge(s_gp, tt + 1)
            v.tensor_scalar(rowb[0:1, 512:1024], rowb[0:1, 0:512], 1e-5, None, ALU.max)
            v.reciprocal(rowb[0:1, 1024:1536], rowb[0:1, 512:1024])
            v.tensor_scalar(rowb[0:1, 1536:2048], rowb[0:1, 1024:1536], 127.0,
                            None, ALU.mult)
            v.tensor_scalar(rowb[0:1, 2048:2560], rowb[0:1, 512:1024],
                            sinvs[0:1, 0:1], 1.0 / 127.0, ALU.mult,
                            ALU.mult).then_inc(s_dv, 1)
            v.wait_ge(s_pe, pe_m(tt, 2))
            v.tensor_copy(sxi[:, :], pbc2[:, :])
            for j in range(8):
                v.tensor_tensor(qf[:, :], pbc1[:, :], xb[:, j * 512:(j + 1) * 512],
                                ALU.mult)
                v.tensor_scalar(qf[:, :], qf[:, :], MAGIC, None, ALU.add)
                op = v.tensor_scalar(xq[:, j * 512:(j + 1) * 512], qf[:, :],
                                     -MAGIC, None, ALU.add)
                if j == 7:
                    op.then_inc(s_dv, 1)
            for oc in range(32):
                v.wait_ge(s_pe, pe_m(tt, 3 + oc))
                if oc >= 2:
                    v.wait_ge(s_st, 16 * (tt * 32 + oc - 1))
                elif tt > 0:
                    v.wait_ge(s_st, 16 * ((tt - 1) * 32 + 31 + oc))
                psh = ph1 if oc % 2 == 0 else ph2
                v.tensor_tensor(hb[:, (oc % 2) * 512:(oc % 2) * 512 + 512],
                                psh[:, :], sxi[:, :], ALU.mult).then_inc(s_dv, 1)

    @blk.tensor
    def _(t):
        for tt in range(NT):
            t.wait_ge(s_dv, dv_m(tt, 2))
            mm = t.matmul(pbc1[:, :], orf[0:1, :], rowb[0:1, 1536:2048],
                          start=True, stop=True)
            mm.then_inc(s_pe, 1)
            mm = t.matmul(pbc2[:, :], orf[0:1, :], rowb[0:1, 2048:2560],
                          start=True, stop=True)
            mm.then_inc(s_pe, 1)
            t.wait_ge(s_dv, dv_m(tt, 3))
            for oc in range(32):
                if oc >= 2:
                    t.wait_ge(s_dv, dv_m(tt, 4 + oc - 2))
                elif tt > 0:
                    t.wait_ge(s_dv, dv_m(tt - 1, 34 + oc))
                psh = ph1 if oc % 2 == 0 else ph2
                for j in range(8):
                    mm = t.matmul(psh[:, :],
                                  w1qs[:, j * 4096 + oc * 128:j * 4096 + oc * 128 + 128],
                                  xq[:, j * 512:(j + 1) * 512],
                                  start=(j == 0), stop=(j == 7))
                    if j == 7:
                        mm.then_inc(s_pe, 1)

    @blk.sync
    def _(s):
        for tt in range(NT):
            t0 = tt * 512
            for oc in range(32):
                s.wait_ge(s_dv, dv_m(tt, 4 + oc))
                s.dma_start(out=h_dt[oc * 128:(oc + 1) * 128, t0:t0 + 512],
                            in_=hb[:, (oc % 2) * 512:(oc % 2) * 512 + 512]
                            ).then_inc(s_st, 16)
        s.wait_ge(s_st, 16 * NT * 32)

    p.ctx.close()
    return p.nc
def build_l1_c1(T=4096):
    p = Prog("l1_c1")
    NT = T // 512
    h_dt = p.inp("h_dt", [4096, T], BF16)
    w2qT = p.inp("w2qT", [4096, 1024], BF16)
    wcp2 = p.inp("wcp2", [128, 96])
    bcp2 = p.inp("bcp2", [128, 32])
    sinv = p.inp("sinv", [1, 2])
    onesrowf = p.inp("onesrowf", [1, 128])
    fT = p.out("fT", [1024, T])

    w2qs = p.sb("w2qs", [128, 32 * 1024], BF16)
    wcs = p.sb("wcs", [128, 96], F32)
    bcs = p.sb("bcs", [128, 32], F32)
    sinvs = p.sb("sinvs", [1, 2], F32)
    orf = p.sb("orf", [1, 128], F32)
    hc = p.sb("hc", [128, 2 * 514], F32)
    tm1 = p.sb("tm1", [128, 512], F32)
    tm2 = p.sb("tm2", [128, 512], F32)
    yf = p.sb("yf", [128, 512], F32)
    af = p.sb("af", [128, 512], F32)
    macc = p.sb("macc", [128, 512], F32)
    hfb = p.sb("hfb", [128, 32 * 512], BF16)
    hqb = p.sb("hqb", [128, 32 * 512], BF16)
    rowb = p.sb("rowb", [1, 5 * 512], F32)
    sxi = p.sb("sxi", [128, 512], F32)
    qf = p.sb("qf", [128, 512], F32)
    qg = p.sb("qg", [128, 512], F32)
    fb = p.sb("fb", [128, 2 * 512], F32)
    ph1 = p.ps("ph1", [128, 512])
    ph2 = p.ps("ph2", [128, 512])
    pbc1 = p.ps("pbc1", [128, 512])
    pbc2 = p.ps("pbc2", [128, 512])

    s_ld = p.sem("s_ld")
    s_pe = p.sem("s_pe")
    s_dv = p.sem("s_dv")
    s_ac = p.sem("s_ac")
    s_st = p.sem("s_st")
    s_gp = p.sem("s_gp")
    blk = p.block()
    NC0 = 36
    DV_T = 74
    AC_T = 32
    PE_T = 2 + 8

    def dv_m(tt, e):
        return tt * DV_T + e

    def ac_m(tt, e):
        return tt * AC_T + e

    def pe_m(tt, e):
        return tt * PE_T + e

    @blk.gpsimd
    def _(g):
        for j in range(32):
            g.dma_start(out=w2qs[:, j * 1024:(j + 1) * 1024],
                        in_=w2qT[j * 128:(j + 1) * 128, :]).then_inc(s_ld, 16)
        g.dma_start(out=wcs[:], in_=wcp2).then_inc(s_ld, 16)
        g.dma_start(out=bcs[:], in_=bcp2).then_inc(s_ld, 16)
        g.dma_start(out=sinvs[:], in_=sinv).then_inc(s_ld, 16)
        g.dma_start(out=orf[:], in_=onesrowf).then_inc(s_ld, 16)
        for tt in range(NT):
            t0 = tt * 512
            for rt in range(32):
                ci = tt * 32 + rt
                if ci >= 2:
                    g.wait_ge(s_dv, dv_m((ci - 2) // 32, ((ci - 2) % 32) * 2 + 1))
                par = ci % 2
                lo = max(t0 - 1, 0)
                hi = min(t0 + 513, T)
                off = 1 if t0 == 0 else 0
                g.dma_start(out=hc[:, par * 514 + off:par * 514 + off + (hi - lo)],
                            in_=h_dt[rt * 128:(rt + 1) * 128, lo:hi]).then_inc(s_ld, 16)
            g.wait_ge(s_dv, dv_m(tt, 64))
            g.reduce_max(rowb[0:1, 0:512], macc[:, :],
                         axis=mybir.AxisListType.C).then_inc(s_gp, 1)

    @blk.vector
    def _(v):
        for tt in range(NT):
            t0 = tt * 512
            for rt in range(32):
                ci = tt * 32 + rt
                par = ci % 2
                xb0 = par * 514
                v.wait_ge(s_ld, 16 * (NC0 + ci + 1))
                if ci >= 1:
                    v.wait_ge(s_ac, ((ci - 1) // 32) * AC_T + (ci - 1) % 32 + 1)
                if t0 == 0:
                    v.memset(hc[:, xb0:xb0 + 1], 0.0)
                if t0 + 513 > T:
                    v.memset(hc[:, xb0 + 513:xb0 + 514], 0.0)
                v.tensor_scalar(tm1[:, :], hc[:, xb0:xb0 + 512],
                                wcs[:, rt * 3:rt * 3 + 1], None, ALU.mult)
                v.scalar_tensor_tensor(tm2[:, :], hc[:, xb0 + 1:xb0 + 513],
                                       wcs[:, rt * 3 + 1:rt * 3 + 2], tm1[:, :],
                                       ALU.mult, ALU.add)
                v.scalar_tensor_tensor(tm1[:, :], hc[:, xb0 + 2:xb0 + 514],
                                       wcs[:, rt * 3 + 2:rt * 3 + 3], tm2[:, :],
                                       ALU.mult, ALU.add).then_inc(s_dv, 1)
                v.wait_ge(s_ac, ac_m(tt, rt + 1))
                v.tensor_copy(hfb[:, rt * 512:(rt + 1) * 512], yf[:, :])
                if rt == 0:
                    op = v.tensor_copy(macc[:, :], af[:, :])
                else:
                    op = v.tensor_tensor(macc[:, :], macc[:, :], af[:, :], ALU.max)
                op.then_inc(s_dv, 1)
            v.wait_ge(s_gp, tt + 1)
            v.tensor_scalar(rowb[0:1, 512:1024], rowb[0:1, 0:512], 1e-5, None, ALU.max)
            v.reciprocal(rowb[0:1, 1024:1536], rowb[0:1, 512:1024])
            v.tensor_scalar(rowb[0:1, 1536:2048], rowb[0:1, 1024:1536], 127.0,
                            None, ALU.mult)
            v.tensor_scalar(rowb[0:1, 2048:2560], rowb[0:1, 512:1024],
                            sinvs[0:1, 1:2], 1.0 / 127.0, ALU.mult,
                            ALU.mult).then_inc(s_dv, 1)
            v.wait_ge(s_pe, pe_m(tt, 2))
            v.tensor_copy(sxi[:, :], pbc2[:, :])
            for rt in range(32):
                v.tensor_copy(qg[:, :], hfb[:, rt * 512:(rt + 1) * 512])
                v.tensor_tensor(qf[:, :], pbc1[:, :], qg[:, :], ALU.mult)
                v.tensor_scalar(qg[:, :], qf[:, :], MAGIC, None, ALU.add)
                op = v.tensor_scalar(hqb[:, rt * 512:(rt + 1) * 512], qg[:, :],
                                     -MAGIC, None, ALU.add)
                if rt == 31:
                    op.then_inc(s_dv, 1)
            for oc in range(8):
                v.wait_ge(s_pe, pe_m(tt, 3 + oc))
                if oc >= 2:
                    v.wait_ge(s_st, 16 * (tt * 8 + oc - 1))
                elif tt > 0:
                    v.wait_ge(s_st, 16 * ((tt - 1) * 8 + 7 + oc))
                psh = ph1 if oc % 2 == 0 else ph2
                v.tensor_tensor(fb[:, (oc % 2) * 512:(oc % 2) * 512 + 512],
                                psh[:, :], sxi[:, :], ALU.mult).then_inc(s_dv, 1)

    @blk.scalar
    def _(a):
        for tt in range(NT):
            for rt in range(32):
                ci = tt * 32 + rt
                a.wait_ge(s_dv, dv_m(ci // 32, (ci % 32) * 2 + 1))
                if ci >= 1:
                    a.wait_ge(s_dv, dv_m((ci - 1) // 32, ((ci - 1) % 32) * 2 + 2))
                a.activation(yf[:, :], tm1[:, :], AF.Silu,
                             bias=bcs[:, rt:rt + 1], scale=1.0)
                a.activation(af[:, :], yf[:, :], AF.Abs).then_inc(s_ac, 1)

    @blk.tensor
    def _(t):
        for tt in range(NT):
            t.wait_ge(s_dv, dv_m(tt, 65))
            mm = t.matmul(pbc1[:, :], orf[0:1, :], rowb[0:1, 1536:2048],
                          start=True, stop=True)
            mm.then_inc(s_pe, 1)
            mm = t.matmul(pbc2[:, :], orf[0:1, :], rowb[0:1, 2048:2560],
                          start=True, stop=True)
            mm.then_inc(s_pe, 1)
            t.wait_ge(s_dv, dv_m(tt, 66))
            for oc in range(8):
                if oc >= 2:
                    t.wait_ge(s_dv, dv_m(tt, 67 + oc - 2))
                elif tt > 0:
                    t.wait_ge(s_dv, dv_m(tt - 1, 73 + oc))
                psh = ph1 if oc % 2 == 0 else ph2
                for j in range(32):
                    mm = t.matmul(psh[:, :],
                                  w2qs[:, j * 1024 + oc * 128:j * 1024 + oc * 128 + 128],
                                  hqb[:, j * 512:(j + 1) * 512],
                                  start=(j == 0), stop=(j == 31))
                    if j == 31:
                        mm.then_inc(s_pe, 1)

    @blk.sync
    def _(s):
        for tt in range(NT):
            t0 = tt * 512
            for oc in range(8):
                s.wait_ge(s_dv, dv_m(tt, 67 + oc))
                s.dma_start(out=fT[oc * 128:(oc + 1) * 128, t0:t0 + 512],
                            in_=fb[:, (oc % 2) * 512:(oc % 2) * 512 + 512]
                            ).then_inc(s_st, 16)
        s.wait_ge(s_st, 16 * NT * 8)

    p.ctx.close()
    return p.nc


def build_addln(T=4096, name="addln"):
    p = Prog(name)
    NT = T // 512
    fT = p.inp("fT", [1024, T])
    xT = p.inp("xT", [1024, T])
    onescol = p.inp("onescol", [1024, 1])
    onesrow = p.inp("onesrow", [1, 128])
    x4T = p.out("x4T", [1024, T])

    xb = p.sb("xb", [128, 8 * 512], F32)
    fbs = p.sb("fbs", [128, 8 * 512], F32)
    xr = p.sb("xr", [128, 8 * 512], F32)
    xrb = p.sb("xrb", [128, 8 * 512], BF16)
    sqb = p.sb("sqb", [128, 8 * 512], BF16)
    ocol = p.sb("ocol", [128, 1], BF16)
    orow = p.sb("orow", [1, 128], F32)
    srow = p.sb("srow", [1, 6 * 512], F32)
    tmp = p.sb("tmp", [128, 8 * 512], F32)
    x1s = p.sb("x1s", [128, 8 * 512], F32)
    pd_s = p.ps("pd_s", [128, 512])
    pd_q = p.ps("pd_q", [128, 512])
    ps_b1 = p.ps("ps_b1", [128, 512])
    ps_b2 = p.ps("ps_b2", [128, 512])

    s_ld = p.sem("s_ld")
    s_pe = p.sem("s_pe")
    s_dv = p.sem("s_dv")
    s_ac = p.sem("s_ac")
    s_st = p.sem("s_st")
    blk = p.block()
    DV_T = 12
    AC_T = 12
    PE_T = 4

    def dv_m(tt, e):
        return tt * DV_T + e

    def ac_m(tt, e):
        return tt * AC_T + e

    def pe_m(tt, e):
        return tt * PE_T + e

    @blk.gpsimd
    def _(g):
        g.dma_start(out=ocol[:, 0:1], in_=onescol[0:128, :]).then_inc(s_ld, 16)
        g.dma_start(out=orow[:], in_=onesrow).then_inc(s_ld, 16)
        for tt in range(NT):
            t0 = tt * 512
            if tt >= 1:
                g.wait_ge(s_dv, dv_m(tt - 1, 12))
            for j in range(8):
                g.dma_start(out=xb[:, j * 512:(j + 1) * 512],
                            in_=xT[j * 128:(j + 1) * 128, t0:t0 + 512]).then_inc(s_ld, 16)
            for j in range(8):
                g.dma_start(out=fbs[:, j * 512:(j + 1) * 512],
                            in_=fT[j * 128:(j + 1) * 128, t0:t0 + 512]).then_inc(s_ld, 16)

    @blk.vector
    def _(v):
        for tt in range(NT):
            v.wait_ge(s_ld, 16 * (2 + 16 * (tt + 1)))
            for j in range(8):
                op = v.tensor_tensor(xr[:, j * 512:(j + 1) * 512],
                                     fbs[:, j * 512:(j + 1) * 512],
                                     xb[:, j * 512:(j + 1) * 512], ALU.add)
                if j == 7:
                    op.then_inc(s_dv, 1)
            for j in range(8):
                op = v.tensor_copy(xrb[:, j * 512:(j + 1) * 512],
                                   xr[:, j * 512:(j + 1) * 512])
                if j == 7:
                    op.then_inc(s_dv, 1)
            v.wait_ge(s_pe, pe_m(tt, 2))
            v.wait_ge(s_ac, ac_m(tt, 9))
            v.tensor_tensor(srow[0:1, 2 * 512:2 * 512 + 512], pd_q[0:1, :],
                            srow[0:1, 5 * 512:5 * 512 + 512], ALU.subtract)
            v.tensor_scalar(srow[0:1, 2 * 512:2 * 512 + 512],
                            srow[0:1, 2 * 512:2 * 512 + 512], 1e-5, None,
                            ALU.add).then_inc(s_dv, 1)
            v.wait_ge(s_ac, ac_m(tt, 12))
            v.tensor_scalar(srow[0:1, 4 * 512:4 * 512 + 512],
                            srow[0:1, 0 * 512:0 * 512 + 512], -1.0, None, ALU.mult)
            v.tensor_tensor(srow[0:1, 4 * 512:4 * 512 + 512],
                            srow[0:1, 4 * 512:4 * 512 + 512],
                            srow[0:1, 3 * 512:3 * 512 + 512], ALU.mult).then_inc(s_dv, 1)
            v.wait_ge(s_pe, pe_m(tt, 4))
            for j in range(8):
                if tt > 0:
                    v.wait_ge(s_st, 16 * ((tt - 1) * 8 + j + 1))
                v.tensor_tensor(tmp[:, j * 512:(j + 1) * 512], ps_b1[:, :],
                                xr[:, j * 512:(j + 1) * 512], ALU.mult)
                op = v.tensor_tensor(x1s[:, j * 512:(j + 1) * 512],
                                     tmp[:, j * 512:(j + 1) * 512], ps_b2[:, :],
                                     ALU.add)
                op.then_inc(s_dv, 1)

    @blk.tensor
    def _(t):
        for tt in range(NT):
            t.wait_ge(s_dv, dv_m(tt, 2))
            if tt > 0:
                t.wait_ge(s_ac, ac_m(tt - 1, 10))
            for dj in range(8):
                mm = t.matmul(pd_s[0:1, :], ocol[:, 0:1], xrb[:, dj * 512:(dj + 1) * 512],
                              start=(dj == 0), stop=(dj == 7))
                if dj == 7:
                    mm.then_inc(s_pe, 1)
            t.wait_ge(s_ac, ac_m(tt, 8))
            if tt > 0:
                t.wait_ge(s_dv, dv_m(tt - 1, 3))
            for dj in range(8):
                mm = t.matmul(pd_q[0:1, :], ocol[:, 0:1], sqb[:, dj * 512:(dj + 1) * 512],
                              start=(dj == 0), stop=(dj == 7))
                if dj == 7:
                    mm.then_inc(s_pe, 1)
            t.wait_ge(s_ac, ac_m(tt, 12))
            if tt > 0:
                t.wait_ge(s_dv, dv_m(tt - 1, 12))
            mm = t.matmul(ps_b1[:, :], orow[:, :], srow[0:1, 3 * 512:3 * 512 + 512],
                          start=True, stop=True)
            mm.then_inc(s_pe, 1)
            t.wait_ge(s_dv, dv_m(tt, 4))
            mm = t.matmul(ps_b2[:, :], orow[:, :], srow[0:1, 4 * 512:4 * 512 + 512],
                          start=True, stop=True)
            mm.then_inc(s_pe, 1)

    @blk.scalar
    def _(a):
        for tt in range(NT):
            a.wait_ge(s_dv, dv_m(tt, 2))
            if tt > 0:
                a.wait_ge(s_pe, pe_m(tt - 1, 2))
            for j in range(8):
                op = a.activation(sqb[:, j * 512:(j + 1) * 512],
                                  xrb[:, j * 512:(j + 1) * 512], AF.Square)
                if j == 7:
                    op.then_inc(s_ac, 8)
            a.wait_ge(s_pe, pe_m(tt, 1))
            a.activation(srow[0:1, 5 * 512:5 * 512 + 512], pd_s[0:1, :],
                         AF.Square).then_inc(s_ac, 1)
            a.copy(srow[0:1, 0 * 512:0 * 512 + 512], pd_s[0:1, :]).then_inc(s_ac, 1)
            a.wait_ge(s_dv, dv_m(tt, 3))
            a.activation(srow[0:1, 1 * 512:1 * 512 + 512],
                         srow[0:1, 2 * 512:2 * 512 + 512], AF.Ln).then_inc(s_ac, 1)
            a.activation(srow[0:1, 3 * 512:3 * 512 + 512],
                         srow[0:1, 1 * 512:1 * 512 + 512], AF.Exp,
                         scale=-0.5).then_inc(s_ac, 1)

    @blk.sync
    def _(s):
        for tt in range(NT):
            t0 = tt * 512
            for j in range(8):
                s.wait_ge(s_dv, dv_m(tt, 4 + j + 1))
                s.dma_start(out=x4T[j * 128:(j + 1) * 128, t0:t0 + 512],
                            in_=x1s[:, j * 512:(j + 1) * 512]).then_inc(s_st, 16)
        s.wait_ge(s_st, 16 * NT * 8)

    p.ctx.close()
    return p.nc


import ml_dtypes
from concourse.bass_utils import run_bass_kernel_spmd

_T = 4096
_C = 32
_NCORE = 8
_progs = {}


def _prog(name, fn):
    if name not in _progs:
        _progs[name] = fn()
    return _progs[name]


def kernel(**inputs):
    T = _T
    f32 = np.float32
    bf = ml_dtypes.bfloat16
    x = np.asarray(inputs["L0_attn_x"], f32)
    cores = list(range(_NCORE))

    onescol = np.full((1024, 1), 1.0 / 1024, f32)
    onesrow = np.ones((1, 128), f32)
    onesrow1 = np.ones((1, 128), f32)
    onescolf = np.ones((128, 1), f32)
    identf = np.eye(128, dtype=f32)
    identb = np.eye(128, dtype=bf)

    TS = 512
    dps = [host_l0_proj_inputs(x[c], inputs["L0_attn_Wq"], inputs["L0_attn_Wk"],
                               inputs["L0_attn_Wv"], inputs["L0_attn_Wg"],
                               inputs["L0_attn_Wa"], inputs["L0_attn_ba"],
                               inputs["L0_attn_Wb"], inputs["L0_attn_bb"],
                               T=TS, C=_C) for c in cores]
    xTfull = [np.ascontiguousarray(x[c].T) for c in cores]
    slices = []
    for sl in range(T // TS):
        dd = []
        for c in cores:
            d = dict(dps[c])
            d["xT"] = np.ascontiguousarray(xTfull[c][:, sl * TS:(sl + 1) * TS])
            dd.append(d)
        slices.append(run_bass_kernel_spmd(
            _prog("l0_proj", lambda: build_l0_proj(T=TS, C=_C)),
            dd, core_ids=cores).results)
    r1 = []
    for c in cores:
        rc = {}
        for k in ("q_dt", "k_dt", "g_dt"):
            rc[k] = np.concatenate([np.asarray(slices[sl][c][k])
                                    for sl in range(T // TS)], axis=1)
        for k in ("kt_td", "vt_td", "fT"):
            rc[k] = np.concatenate([np.asarray(slices[sl][c][k])
                                    for sl in range(T // TS)], axis=0)
        rc["gamC"] = np.concatenate([np.asarray(slices[sl][c]["gamC"])
                                     for sl in range(T // TS)], axis=1)
        r1.append(rc)

    scon = host_l0_scan_consts(T=T, C=_C)
    d2s = []
    for c in cores:
        d = dict(scon)
        for k in ("q_dt", "k_dt", "kt_td", "vt_td", "fT", "gamC"):
            d[k] = np.asarray(r1[c][k])
        d2s.append(d)
    r2 = run_bass_kernel_spmd(_prog("l0_scan", lambda: build_l0_scan(T=T, C=_C)),
                              d2s, core_ids=cores).results

    woT0 = np.ascontiguousarray(inputs["L0_attn_Wo"].T).astype(f32)
    d3s = [dict(oT=np.asarray(r2[c]["oT"]), g_dt=np.asarray(r1[c]["g_dt"]),
                xT=xTfull[c].astype(f32), woT=woT0, onescol=onescol,
                onesrow=onesrow) for c in cores]
    r3 = run_bass_kernel_spmd(_prog("l0_post", lambda: build_l0_post(T=T)),
                              d3s, core_ids=cores).results

    w1T0 = np.ascontiguousarray(inputs["L0_ffn_W1"].T).astype(f32)
    w2T0 = np.ascontiguousarray(inputs["L0_ffn_W2"].T).astype(f32)
    b1c0 = np.asarray(inputs["L0_ffn_b1"], f32).reshape(4096, 1)
    b2c0 = np.asarray(inputs["L0_ffn_b2"], f32).reshape(1024, 1)
    scl0 = np.full((128, 1), float(np.asarray(inputs["L0_ffn_scale"])), f32)
    d4s = [dict(x1T=np.asarray(r3[c]["x1T"]), x1b=np.asarray(r3[c]["x1b"]),
                w1T=w1T0, w2T=w2T0, b1c=b1c0, b2c=b2c0, scl=scl0,
                onescol=onescol, onesrow=onesrow, ident=identf)
           for c in cores]
    r4 = run_bass_kernel_spmd(_prog("l0_ffn", lambda: build_l0_ffn(T=T)),
                              d4s, core_ids=cores).results

    wqkvT = np.ascontiguousarray(np.concatenate(
        [inputs["L1_attn_Wq"].T, inputs["L1_attn_Wk"].T,
         inputs["L1_attn_Wv"].T], axis=1)).astype(f32)
    waT = np.ascontiguousarray(inputs["L1_attn_Wa"].T).astype(f32)
    wbT = np.ascontiguousarray(inputs["L1_attn_Wb"].T).astype(f32)
    wgT = np.ascontiguousarray(inputs["L1_attn_Wg"].T).astype(f32)
    ba_r = np.asarray(inputs["L1_attn_ba"], f32).reshape(1, 1024)
    bb_r = np.asarray(inputs["L1_attn_bb"], f32).reshape(1, 8)
    d5s = [dict(xbi=np.asarray(r4[c]["x2b"]), wqkvT=wqkvT, waT=waT, wbT=wbT,
                wgT=wgT, ba_r=ba_r, bb_r=bb_r, onesrow1=onesrow1)
           for c in cores]
    r5 = run_bass_kernel_spmd(_prog("l1_proj", lambda: build_l1_proj(T=T)),
                              d5s, core_ids=cores).results

    wall = np.concatenate([inputs["L1_attn_qc_w"], inputs["L1_attn_kc_w"],
                           inputs["L1_attn_vc_w"]], axis=0)[:, 0, :]
    ball = np.concatenate([inputs["L1_attn_qc_b"], inputs["L1_attn_kc_b"],
                           inputs["L1_attn_vc_b"]])
    wcp = np.zeros((128, 64), f32)
    bcp = np.zeros((128, 16), f32)
    for rt in range(16):
        wcp[:, rt * 4:rt * 4 + 4] = wall[rt * 128:(rt + 1) * 128]
        bcp[:, rt] = ball[rt * 128:(rt + 1) * 128]
    cumc = np.zeros((128, 128), f32)
    for s in range(128):
        for t in range(128):
            if s <= t and s // _C == t // _C:
                cumc[s, t] = 1.0
    d6s = [dict(qkv_dt=np.asarray(r5[c]["qkv_dt"]), aT=np.asarray(r5[c]["aT"]),
                bT=np.asarray(r5[c]["bT"]), wcp=wcp, bcp=bcp, cumc=cumc,
                identb=identb) for c in cores]
    r6 = run_bass_kernel_spmd(_prog("l1_conv", lambda: build_l1_conv(T=T)),
                              d6s, core_ids=cores).results

    mkS = np.zeros((128, 128), f32)
    mkI = np.zeros((128, 128), f32)
    for s in range(128):
        for t in range(128):
            if s // 32 == t // 32:
                if s < t:
                    mkS[s, t] = 1
                if s <= t:
                    mkI[s, t] = 1
    selz = np.zeros((8, 512), f32)
    for q in range(4):
        for pp in range(128):
            selz[2 * q + (pp >= 64), q * 128 + pp] = 1
    lnwx = np.tile(np.asarray(inputs["L1_attn_ln_w"], f32)[None, :], (128, 1))
    lnbx = np.tile(np.asarray(inputs["L1_attn_ln_b"], f32)[None, :], (128, 1))
    d7s = [dict(q_dt=np.asarray(r6[c]["q_dt"]), k_dt=np.asarray(r6[c]["k_dt"]),
                kt_td=np.asarray(r6[c]["kt_td"]), vt_td=np.asarray(r6[c]["vt_td"]),
                gamT=np.asarray(r6[c]["gamT"]), invgT=np.asarray(r6[c]["invgT"]),
                np2T=np.asarray(r6[c]["np2T"]), bT=np.asarray(r5[c]["bT"]),
                gT=np.asarray(r5[c]["gT"]), mkS=mkS, mkI=mkI, selz=selz,
                lnwx=lnwx, lnbx=lnbx) for c in cores]
    r7 = run_bass_kernel_spmd(_prog("l1_scan", lambda: build_l1_scan(T=T)),
                              d7s, core_ids=cores).results

    woT1 = np.ascontiguousarray(inputs["L1_attn_Wo"].T).astype(f32)
    gdummy = np.zeros((1024, T), bf)
    d8s = [dict(oT=np.asarray(r7[c]["oT"]), g_dt=gdummy,
                xT=np.asarray(r4[c]["x2T"]), woT=woT1, onescol=onescol,
                onesrow=onesrow) for c in cores]
    r8 = run_bass_kernel_spmd(
        _prog("l1_post", lambda: build_l0_post(T=T, name="l1_post", gate=False)),
        d8s, core_ids=cores).results

    fw1T = np.ascontiguousarray(inputs["L1_ffn_w1"].T).astype(f32)
    fw2T = np.ascontiguousarray(inputs["L1_ffn_w2"].T).astype(f32)
    d9 = dict(w1T=fw1T, w2T=fw2T, onescolf=onescolf, onesrowf=onesrow)
    r9 = run_bass_kernel_spmd(_prog("l1_fw", build_l1_fw),
                              [d9 for _ in cores], core_ids=cores).results

    d10s = [dict(x3T=np.asarray(r8[c]["x1T"]), w1qT=np.asarray(r9[c]["w1qT"]),
                 sinv=np.asarray(r9[c]["sinv"]), onesrowf=onesrow)
            for c in cores]
    r10 = run_bass_kernel_spmd(_prog("l1_h", lambda: build_l1_h(T=T)),
                               d10s, core_ids=cores).results

    cw2 = np.asarray(inputs["L1_ffn_conv_w"], f32)[:, 0, :]
    cb2 = np.asarray(inputs["L1_ffn_conv_b"], f32)
    wcp2 = np.zeros((128, 96), f32)
    bcp2 = np.zeros((128, 32), f32)
    for rt in range(32):
        wcp2[:, rt * 3:rt * 3 + 3] = cw2[rt * 128:(rt + 1) * 128]
        bcp2[:, rt] = cb2[rt * 128:(rt + 1) * 128]
    d11s = [dict(h_dt=np.asarray(r10[c]["h_dt"]), w2qT=np.asarray(r9[c]["w2qT"]),
                 wcp2=wcp2, bcp2=bcp2, sinv=np.asarray(r9[c]["sinv"]),
                 onesrowf=onesrow) for c in cores]
    r11 = run_bass_kernel_spmd(_prog("l1_c1", lambda: build_l1_c1(T=T)),
                               d11s, core_ids=cores).results

    d12s = [dict(fT=np.asarray(r11[c]["fT"]), xT=np.asarray(r8[c]["x1T"]),
                 onescol=onescol, onesrow=onesrow) for c in cores]
    r12 = run_bass_kernel_spmd(_prog("addln", lambda: build_addln(T=T)),
                               d12s, core_ids=cores).results

    out = np.stack([np.ascontiguousarray(np.asarray(r12[c]["x4T"]).T)
                    for c in cores], axis=0)
    return out.astype(np.float32)
